# version 02
# baseline (speedup 1.0000x reference)
import math
import numpy as np
import ml_dtypes
from contextlib import ExitStack
import concourse.bass as bass
import concourse.mybir as mybir
from concourse.bass_utils import run_bass_kernel_spmd

F32 = mybir.dt.float32
BF16 = mybir.dt.bfloat16
I8 = mybir.dt.int8
AF = mybir.ActivationFunctionType
ALU = mybir.AluOpType
AX = mybir.AxisListType

S = 2048
D = 2048
NCH = 16
NS = 16
NT = 4
INTER = 8192
NIC = 64
SCALE = 1.0 / math.sqrt(128.0)
EPS = 1e-5

ARENA_COLS = 36864
A_Q4, A_K4, A_V4, A_PT, A_PR, A_CS = 0, 8192, 16384, 24576, 32768, 34816
B_Q4, B_KB, B_V4, B_PT, B_PR, B_CS = 0, 8192, 10240, 18432, 26624, 28672
F_HT = 0
XN_TM = 32768
OP_CT = 0


def build_program():
    nc = bass.Bass("TRN2", target_bir_lowering=False)

    def din(name, shape, dt):
        return nc.dram_tensor(name, shape, dt, kind="ExternalInput").ap()

    x_d = din("x", [S, D], F32)
    wqt_d = din("wqt", [D, D], F32)
    wkt_d = din("wkt", [D, D], F32)
    wvt_d = din("wvt", [D, D], F32)
    wot_d = din("wot", [D, D], F32)
    wattn_d = din("wattn", [D, 4608], F32)
    wout_d = din("wout", [INTER, D], F32)
    wg8_d = din("wg8", [D, INTER], I8)
    wu8_d = din("wu8", [D, INTER], I8)
    wd8_d = din("wd8", [INTER, D], I8)
    sgc_d = din("sgc", [128, 64], F32)
    suc_d = din("suc", [128, 64], F32)
    sdbc_d = din("sdbc", [128, D], F32)
    bbc_d = din("bbc", [128, D], F32)
    cosb_d = din("cosb", [128, S], BF16)
    sinb_d = din("sinb", [128, S], BF16)
    mask_d = din("mask01", [128, D], BF16)
    id_d = din("ident", [128, 128], BF16)

    out_d = nc.dram_tensor("out", [S, D], F32, kind="ExternalOutput").ap()
    xres_d = nc.dram_tensor("xres", [S, D], F32, kind="Internal").ap()
    stash_d = nc.dram_tensor("stash", [S, D], F32, kind="Internal").ap()
    ctxa_d = nc.dram_tensor("ctxa", [16, 128, S], BF16, kind="Internal").ap()
    ctxb_d = nc.dram_tensor("ctxb", [64, 128, S], BF16, kind="Internal").ap()
    woutb_d = nc.dram_tensor("woutb", [64, 128, D], BF16, kind="Internal").ap()

    est = ExitStack()
    sems = {}
    SEM_NAMES = (["cN", "eA", "eB", "wb", "rp", "cw", "xw0", "xw1", "pc", "st",
                  "wf0", "wf1", "pf0", "pf1", "sg0", "sg1", "sg2", "sg3"]
                 + [f"wi{j}" for j in range(8)]
                 + ["tS", "vS", "aS", "pS"])
    for k in SEM_NAMES:
        sems[k] = est.enter_context(nc.semaphore(k))
    DMA_SEMS = set(SEM_NAMES) - {"tS", "vS", "aS", "pS"}

    xnf = nc.alloc_sbuf_tensor("xnf", [128, NCH * S], BF16).ap()
    cosb = nc.alloc_sbuf_tensor("sb_cosb", [128, S], BF16).ap()
    sinb = nc.alloc_sbuf_tensor("sb_sinb", [128, S], BF16).ap()
    ident = nc.alloc_sbuf_tensor("sb_ident", [128, 128], BF16).ap()
    mask01 = nc.alloc_sbuf_tensor("sb_mask01", [128, D], BF16).ap()
    sgc = nc.alloc_sbuf_tensor("sb_sgc", [128, 64], F32).ap()
    suc = nc.alloc_sbuf_tensor("sb_suc", [128, 64], F32).ap()
    cols = nc.alloc_sbuf_tensor("cols", [128, 16], F32).ap()
    arena = nc.alloc_sbuf_tensor("arena", [128, ARENA_COLS], BF16).ap()
    ebA = nc.alloc_sbuf_tensor("ebA", [128, D], F32).ap()
    ebB = nc.alloc_sbuf_tensor("ebB", [128, D], F32).ap()
    ropeb = nc.alloc_sbuf_tensor("ropeb", [128, 1024], BF16).ap()
    wbf = nc.alloc_sbuf_tensor("wbf", [128, 16 * 512], BF16).ap()
    wstf = nc.alloc_sbuf_tensor("wstf", [128, 2 * 512], F32).ap()
    wsti = nc.alloc_sbuf_tensor("wsti", [128, 8 * 512], I8).ap()
    pcv = nc.alloc_sbuf_tensor("pcv", [128, 512], BF16).ap()
    pcf = nc.alloc_sbuf_tensor("pcf", [128, 1024], F32).ap()

    scor = nc.alloc_psum_tensor("scor", [128, 2048], F32).ap()
    pA = nc.alloc_psum_tensor("pA", [128, 512], F32).ap()
    pB = nc.alloc_psum_tensor("pB", [128, 512], F32).ap()
    pC = nc.alloc_psum_tensor("pC", [128, 512], F32).ap()
    ptp = nc.alloc_psum_tensor("ptp", [128, 512], BF16).ap()

    streams = {e: [] for e in ("tensor", "vector", "scalar", "gpsimd", "sync")}
    ctr = {k: 0 for k in sems}
    water = {e: {k: 0 for k in sems} for e in streams}

    def flat(*evs):
        out = []
        for e in evs:
            if e is None:
                continue
            if isinstance(e, list):
                out.extend(x for x in e if x is not None)
            else:
                out.append(e)
        return out

    def emit(eng, fn, inc=None, waits=()):
        byname = {}
        for (s_, v) in waits:
            if v > byname.get(s_, 0):
                byname[s_] = v
        ws = []
        for s_, v in byname.items():
            if v > water[eng][s_]:
                ws.append((s_, v))
                water[eng][s_] = v
        n = 0
        if inc is not None:
            n = 16 if inc in DMA_SEMS else 1
            ctr[inc] += n
        streams[eng].append((ws, fn, inc, n))
        return ctr[inc] if inc else None

    def dma(q, sem, out, in_, waits=()):
        return emit(q, lambda e, o=out, i=in_: e.dma_start(out=o, in_=i), inc=sem, waits=waits)

    ev = {
        "xnf": [], "ebA_free": [], "ebB_free": [], "ptp_free": [], "ln_free": [],
        "ropeb_free": [], "m1_free": [], "m2_free": [], "qkv": [], "ctxs_free": [],
        "opct_free": [], "consts": [], "hT": [], "ebA_bias": [],
    }

    wbf_free = [[] for _ in range(16)]
    wbf_ready = [[] for _ in range(16)]
    wstf_free = [[], []]
    wsti_free = [[] for _ in range(8)]
    wstf_next = [0]
    wsti_next = [0]
    pcv_free = [[]]

    def load_w_f32(slot, dram_tile, w=512, extra=()):
        j = wstf_next[0]; wstf_next[0] = (j + 1) % 2
        c = dma("sync", f"wf{j}", wstf[:, j * 512:j * 512 + w], dram_tile,
                waits=flat(wstf_free[j], list(extra)))
        cc = emit("scalar", lambda e, s=slot, jj=j, w_=w: e.copy(
            out=wbf[:, s * 512:s * 512 + w_], in_=wstf[:, jj * 512:jj * 512 + w_]),
            inc="aS", waits=flat([(f"wf{j}", c)], wbf_free[slot]))
        wstf_free[j] = [("aS", cc)]
        wbf_ready[slot] = [("aS", cc)]

    def load_w_i8(slot, dram_tile, extra=()):
        j = wsti_next[0]; wsti_next[0] = (j + 1) % 8
        c = dma("sync", f"wi{j}", wsti[:, j * 512:(j + 1) * 512], dram_tile,
                waits=flat(wsti_free[j], list(extra)))
        cc = emit("gpsimd", lambda e, s=slot, jj=j: e.tensor_copy(
            out=wbf[:, s * 512:(s + 1) * 512], in_=wsti[:, jj * 512:(jj + 1) * 512]),
            inc="pS", waits=flat([(f"wi{j}", c)], wbf_free[slot]))
        wsti_free[j] = [("pS", cc)]
        wbf_ready[slot] = [("pS", cc)]

    def load_w_bf16(slot, dram_tile, extra=()):
        c = dma("sync", "wb", wbf[:, slot * 512:(slot + 1) * 512], dram_tile,
                waits=flat(wbf_free[slot], list(extra)))
        wbf_ready[slot] = [("wb", c)]

    dma("sync", "cN", cosb, cosb_d[:, :])
    dma("sync", "cN", sinb, sinb_d[:, :])
    dma("sync", "cN", ident, id_d[:, :])
    dma("sync", "cN", mask01, mask_d[:, :])
    dma("sync", "cN", sgc, sgc_d[:, :])
    c = dma("sync", "cN", suc, suc_d[:, :])
    ev["consts"] = [("cN", c)]

    def ln_chain(s, pre_waits, buf):
        pw = flat(pre_waits, ev["ln_free"])
        v1 = emit("vector", lambda e, b=buf: e.reduce_sum(out=cols[:, 0:1], in_=b, axis=AX.X),
                  inc="vS", waits=pw)
        a1 = emit("scalar", lambda e, b=buf: e.activation(
            out=arena[:, XN_TM:XN_TM + 2048], in_=b, func=AF.Square, accum_out=cols[:, 1:2]),
            inc="aS", waits=pw)
        a2 = emit("scalar", lambda e: e.mul(out=cols[:, 2:3], in_=cols[:, 0:1], mul=1.0 / D),
                  inc="aS", waits=[("vS", v1)])
        v2 = emit("vector", lambda e: e.tensor_scalar(out=cols[:, 3:4], in0=cols[:, 2:3],
                                                      scalar1=cols[:, 2:3], scalar2=-EPS,
                                                      op0=ALU.mult, op1=ALU.add),
                  inc="vS", waits=[("aS", a2)])
        p1 = emit("gpsimd", lambda e: e.tensor_scalar(out=cols[:, 4:5], in0=cols[:, 1:2],
                                                      scalar1=1.0 / D, scalar2=cols[:, 3:4],
                                                      op0=ALU.mult, op1=ALU.subtract),
                  inc="pS", waits=[("aS", a1), ("vS", v2)])
        a3 = emit("scalar", lambda e: e.activation(out=cols[:, 5:6], in_=cols[:, 4:5],
                                                   func=AF.Sqrt),
                  inc="aS", waits=[("pS", p1)])
        v3 = emit("vector", lambda e: e.reciprocal(out=cols[:, 6:7], in_=cols[:, 5:6]),
                  inc="vS", waits=[("aS", a3)])
        p2 = emit("gpsimd", lambda e, b=buf: e.tensor_scalar(
            out=arena[:, XN_TM:XN_TM + 2048], in0=b,
            scalar1=cols[:, 2:3], scalar2=cols[:, 6:7], op0=ALU.subtract, op1=ALU.mult),
            inc="pS", waits=[("vS", v3), ("aS", a1)])
        last_t = 0
        xnfv = xnf.rearrange("a (b c) -> a b c", b=NCH)
        last_v = None
        for g4 in range(4):
            t = None
            for j in range(4):
                kc = g4 * 4 + j
                t = emit("tensor", lambda e, kk=kc, jj=j: e.transpose(
                    out=ptp[:, jj * 128:(jj + 1) * 128],
                    in_=arena[:, XN_TM + kk * 128:XN_TM + (kk + 1) * 128], identity=ident),
                    inc="tS" if j == 3 else None,
                    waits=flat([("pS", p2)], ev["ptp_free"], ev["consts"]) if j == 0 else ())
            last_t = t
            src = ptp.rearrange("a (b c) -> a b c", b=4)
            dst = xnfv[:, g4 * 4:g4 * 4 + 4, s * 128:(s + 1) * 128]
            last_v = emit("vector", lambda e, sr=src, dt_=dst: e.tensor_copy(out=dt_, in_=sr),
                          inc="vS", waits=[("tS", t)])
            ev["ptp_free"] = [("vS", last_v)]
        ev["ln_free"] = [("pS", p2), ("tS", last_t), ("vS", last_v)]
        return last_v

    xnf_last = 0
    for s in range(NS):
        buf, key, sm = (ebA, "ebA_free", "eA") if s % 2 == 0 else (ebB, "ebB_free", "eB")
        c = dma("sync", sm, buf, x_d[s * 128:(s + 1) * 128, :], waits=flat(ev[key], ev["consts"]))
        xnf_last = ln_chain(s, [(sm, c)], buf)
        ev[key] = [("pS", ctr["pS"])]
    ev["xnf"] = [("vS", xnf_last)]

    psum_rot = [pA, pB, pC]
    psum_free = {id(pA): [], id(pB): [], id(pC): []}
    rot_i = [0]

    def next_psum():
        p = psum_rot[rot_i[0] % 3]
        rot_i[0] += 1
        return p

    def proj_accum(dst_psum, stat_slices, rhs_slices, extra_waits=()):
        n = len(stat_slices)
        tlast = None
        for k in range(n):
            tlast = emit("tensor", lambda e, a=stat_slices[k], b=rhs_slices[k], k_=k, n_=n,
                         dp=dst_psum: e.matmul(out=dp[:, 0:512], lhsT=a, rhs=b,
                                               start=(k_ == 0), stop=(k_ == n_ - 1)),
                         inc="tS" if k == n - 1 else None,
                         waits=flat(list(extra_waits), psum_free[id(dst_psum)]) if k == 0 else ())
        return tlast

    def rope_evac(psum, stt, dst_bf, mm_done):
        a1 = emit("scalar", lambda e, p=psum: e.copy(out=ropeb[:, 0:512], in_=p[:, 0:512]),
                  inc="aS", waits=flat([("tS", mm_done)], ev["ropeb_free"]))
        c1 = dma("gpsimd", "rp", ropeb[0:64, 512:1024], ropeb[64:128, 0:512], waits=[("aS", a1)])
        c2 = dma("gpsimd", "rp", ropeb[64:128, 512:1024], ropeb[0:64, 0:512], waits=[("aS", a1)])
        v1 = emit("vector", lambda e, s_=stt: e.tensor_tensor(
            out=ebA[:, 0:512], in0=ropeb[:, 0:512], in1=cosb[:, s_ * 512:(s_ + 1) * 512], op=ALU.mult),
            inc="vS", waits=flat([("aS", a1)], ev["m1_free"]))
        p1 = emit("gpsimd", lambda e, s_=stt: e.tensor_tensor(
            out=ebA[:, 512:1024], in0=ropeb[:, 512:1024], in1=sinb[:, s_ * 512:(s_ + 1) * 512], op=ALU.mult),
            inc="pS", waits=flat([("rp", c2)], ev["m2_free"]))
        v2 = emit("vector", lambda e, d=dst_bf: e.tensor_tensor(
            out=d, in0=ebA[:, 0:512], in1=ebA[:, 512:1024], op=ALU.add),
            inc="vS", waits=[("vS", v1), ("pS", p1)])
        ev["ropeb_free"] = [("vS", v2)]
        ev["m1_free"] = [("vS", v2)]
        ev["m2_free"] = [("vS", v2)]
        return [("aS", a1)], [("vS", v2)]

    probs_free = [[]]
    scor_free = [[]]

    def softmax_transpose(nkt, J, p, is_a, qoff, koff, pt_off, pr_off):
        i = 4 * J + p
        qsl = arena[:, qoff + i * 128: qoff + (i + 1) * 128]
        tlast = None
        for kt in range(nkt):
            tlast = emit("tensor", lambda e, q=qsl, kt_=kt, ko=koff: e.matmul(
                out=scor[:, kt_ * 512:(kt_ + 1) * 512], lhsT=q,
                rhs=arena[:, ko + kt_ * 512: ko + (kt_ + 1) * 512], start=True, stop=True),
                inc="tS" if kt == nkt - 1 else None,
                waits=flat(scor_free[0], ev["qkv"]) if kt == 0 else ())
        L = nkt * 512
        v1 = emit("vector", lambda e, L_=L: e.reduce_max(out=cols[:, 7:8], in_=scor[:, 0:L_], axis=AX.X),
                  inc="vS", waits=[("tS", tlast)])
        v2 = emit("vector", lambda e: e.tensor_scalar(out=cols[:, 8:9], in0=cols[:, 7:8],
                                                      scalar1=-SCALE, scalar2=None, op0=ALU.mult),
                  inc="vS", waits=[("vS", v1)])
        a1 = emit("scalar", lambda e, L_=L, po=pr_off: e.activation(
            out=arena[:, po:po + L_], in_=scor[:, 0:L_], func=AF.Exp,
            bias=cols[:, 8:9], scale=SCALE, accum_out=cols[:, 9:10]),
            inc="aS", waits=flat([("vS", v2)], probs_free[0]))
        scor_free[0] = [("aS", a1)]
        if is_a:
            pm = emit("gpsimd", lambda e, J_=J, p_=p, po=pr_off: e.tensor_tensor(
                out=arena[:, po + J_ * 512: po + (J_ + 1) * 512],
                in0=arena[:, po + J_ * 512: po + (J_ + 1) * 512],
                in1=mask01[:, p_ * 512:(p_ + 1) * 512], op=ALU.mult),
                inc="pS", waits=[("aS", a1)])
            v3 = emit("vector", lambda e, L_=L, po=pr_off: e.reduce_sum(
                out=cols[:, 9:10], in_=arena[:, po:po + L_], axis=AX.X),
                inc="vS", waits=[("pS", pm)])
            v4 = emit("vector", lambda e: e.reciprocal(out=cols[:, 10:11], in_=cols[:, 9:10]),
                      inc="vS", waits=[("vS", v3)])
        else:
            v4 = emit("vector", lambda e: e.reciprocal(out=cols[:, 10:11], in_=cols[:, 9:10]),
                      inc="vS", waits=[("aS", a1)])
        pn = emit("gpsimd", lambda e, L_=L, po=pr_off: e.tensor_scalar(
            out=arena[:, po:po + L_], in0=arena[:, po:po + L_],
            scalar1=cols[:, 10:11], scalar2=None, op0=ALU.mult),
            inc="pS", waits=[("vS", v4)])
        tl_v = None
        for g4 in range(nkt):
            t = None
            for j in range(4):
                kc = g4 * 4 + j
                t = emit("tensor", lambda e, kk=kc, jj=j, po=pr_off: e.transpose(
                    out=ptp[:, jj * 128:(jj + 1) * 128],
                    in_=arena[:, po + kk * 128: po + (kk + 1) * 128], identity=ident),
                    inc="tS" if j == 3 else None,
                    waits=flat([("pS", pn)], ev["ptp_free"]) if j == 0 else ())
            src = ptp.rearrange("a (b c) -> a b c", b=4)
            dstbase = pt_off + g4 * 4 * 512 + p * 128
            dst = arena[:, dstbase: dstbase + 2048].rearrange("a (b c) -> a b c", b=4)[:, :, 0:128]
            tl_v = emit("vector", lambda e, sr=src, dt_=dst: e.tensor_copy(out=dt_, in_=sr),
                        inc="vS", waits=[("tS", t)])
            ev["ptp_free"] = [("vS", tl_v)]
        probs_free[0] = [("vS", tl_v)]

    def attn_a():
        ev["m1_free"] = flat(ev["ebA_free"])
        ev["m2_free"] = flat(ev["ebA_free"])
        for g in range(4):
            qkv_v = []
            for (wd_, kind) in ((wqt_d, "q"), (wkt_d, "k"), (wvt_d, "v")):
                for kc in range(NCH):
                    load_w_f32(kc, wd_[kc * 128:(kc + 1) * 128, g * 512:(g + 1) * 512])
                wrdy = flat(*[wbf_ready[kc] for kc in range(NCH)])
                if kind in ("q", "k"):
                    base = A_Q4 if kind == "q" else A_K4
                    for hh in range(4):
                        for stt in range(NT):
                            pp = next_psum()
                            tl = proj_accum(pp,
                                            [wbf[:, kc * 512 + hh * 128: kc * 512 + hh * 128 + 128] for kc in range(NCH)],
                                            [xnf[:, kc * S + stt * 512: kc * S + (stt + 1) * 512] for kc in range(NCH)],
                                            extra_waits=flat(wrdy, ev["xnf"]))
                            fe, re = rope_evac(pp, stt,
                                               arena[:, base + hh * S + stt * 512: base + hh * S + (stt + 1) * 512], tl)
                            psum_free[id(pp)] = fe
                            qkv_v.append(re[0][1])
                else:
                    for ssub in range(NS):
                        pp = next_psum()
                        tl = proj_accum(pp,
                                        [xnf[:, kc * S + ssub * 128: kc * S + ssub * 128 + 128] for kc in range(NCH)],
                                        [wbf[:, kc * 512:(kc + 1) * 512] for kc in range(NCH)],
                                        extra_waits=flat(wrdy, ev["xnf"]))
                        vv = emit("vector", lambda e, pp_=pp, s_=ssub: e.tensor_copy(
                            out=arena[:, A_V4 + s_ * 512:A_V4 + (s_ + 1) * 512], in_=pp_[:, 0:512]),
                            inc="vS", waits=[("tS", tl)])
                        psum_free[id(pp)] = [("vS", vv)]
                        qkv_v.append(vv)
                for kc in range(NCH):
                    wbf_free[kc] = [("tS", ctr["tS"])]
            ev["qkv"] = [("vS", max(qkv_v))]
            for hh in range(4):
                h = 4 * g + hh
                for J in range(4):
                    for p in range(4):
                        softmax_transpose(J + 1, J, p, True, A_Q4 + hh * S, A_K4 + hh * S, A_PT, A_PR)
                    nkc = (J + 1) * 4
                    pp = next_psum()
                    tl = proj_accum(pp,
                                    [arena[:, A_V4 + kc * 512 + hh * 128: A_V4 + kc * 512 + hh * 128 + 128] for kc in range(nkc)],
                                    [arena[:, A_PT + kc * 512: A_PT + (kc + 1) * 512] for kc in range(nkc)],
                                    extra_waits=flat(probs_free[0], ev["ctxs_free"]))
                    vv = emit("vector", lambda e, pp_=pp, J_=J: e.tensor_copy(
                        out=arena[:, A_CS + J_ * 512: A_CS + (J_ + 1) * 512], in_=pp_[:, 0:512]),
                        inc="vS", waits=[("tS", tl)])
                    psum_free[id(pp)] = [("vS", vv)]
                cw = dma("gpsimd", "cw", ctxa_d[h, :, :], arena[:, A_CS:A_CS + 2048],
                         waits=[("vS", ctr["vS"])])
                ev["ctxs_free"] = [("cw", cw)]
        ev["ebA_free"] = flat(ev["m1_free"], ev["m2_free"])

    stage_free = [[] for _ in range(4)]

    def evac_stash(s_glob, ot, psum_ap, mode, wts):
        slot = (s_glob + ot) % 4
        stg = ebB[:, slot * 512:(slot + 1) * 512]
        w2 = flat(wts, stage_free[slot])
        if mode == "plain":
            vv = emit("vector", lambda e, p=psum_ap, o=stg: e.tensor_copy(out=o, in_=p),
                      inc="vS", waits=w2)
        elif mode == "bias":
            vv = emit("vector", lambda e, p=psum_ap, o=stg, t=ot: e.tensor_tensor(
                out=o, in0=p, in1=ebA[:, t * 512:(t + 1) * 512], op=ALU.add), inc="vS", waits=w2)
        else:
            vv = emit("vector", lambda e, p=psum_ap, o=stg, t=ot: e.tensor_tensor(
                out=o, in0=p, in1=ebA[:, t * 512:(t + 1) * 512], op=ALU.mult), inc="vS", waits=w2)
        cw = dma("gpsimd", f"sg{slot}", stash_d[s_glob * 128:(s_glob + 1) * 128, ot * 512:(ot + 1) * 512],
                 stg, waits=[("vS", vv)])
        stage_free[slot] = [(f"sg{slot}", cw)]

    def run_outproj(NK, mode, mov_loader, src_d, src_ready, bias_d):
        if mode in ("bias", "sdown"):
            cA = dma("sync", "eA", ebA, bias_d[:, :], waits=flat(ev["ebA_free"]))
            ev["ebA_bias"] = [("eA", cA)]
        for ot in range(4):
            for sbi in range(4):
                sb = [sbi * 4 + j for j in range(4)]
                for kb in range(0, NK, 16):
                    nblk = min(16, NK - kb)
                    sevts = None
                    for k in range(nblk):
                        sevts = dma("sync", "st", arena[:, OP_CT + k * 512: OP_CT + (k + 1) * 512],
                                    src_d[kb + k, :, sb[0] * 128: sb[0] * 128 + 512],
                                    waits=flat(ev["opct_free"], src_ready) if k == 0 else ())
                    for k in range(nblk):
                        mov_loader(k, kb + k, ot)
                    wrdy = flat(*[wbf_ready[x] for x in range(nblk)])
                    for j, s_g in enumerate(sb):
                        for k in range(nblk):
                            kk2 = kb + k
                            w_ = flat(wrdy, [("st", sevts)],
                                      scor_free[0] if kb == 0 else []) if (k == 0 and j == 0) else ()
                            emit("tensor", lambda e, j_=j, k_=k, kk_=kk2, sg_=s_g, NK_=NK: e.matmul(
                                out=scor[:, j_ * 512:(j_ + 1) * 512],
                                lhsT=arena[:, OP_CT + k_ * 512 + (sg_ % 4) * 128: OP_CT + k_ * 512 + (sg_ % 4) * 128 + 128],
                                rhs=wbf[:, k_ * 512:(k_ + 1) * 512],
                                start=(kk_ == 0), stop=(kk_ == NK_ - 1)),
                                inc="tS" if (k == nblk - 1 and j == 3) else None,
                                waits=w_)
                    for x in range(nblk):
                        wbf_free[x] = [("tS", ctr["tS"])]
                    ev["opct_free"] = [("tS", ctr["tS"])]
                tl = ctr["tS"]
                for j, s_g in enumerate(sb):
                    wts = [("tS", tl)]
                    if mode in ("bias", "sdown"):
                        wts = flat(wts, ev["ebA_bias"])
                    evac_stash(s_g, ot, scor[:, j * 512:(j + 1) * 512], mode, wts)
                scor_free[0] = [("vS", ctr["vS"])]
        ev["ebA_free"] = [("vS", ctr["vS"])]
        ev["ebB_free"] = [(f"sg{j}", ctr[f"sg{j}"]) for j in range(4)]

    def epilogue(sub):
        src = x_d if sub == 0 else xres_d
        dst = out_d if sub == 3 else xres_d
        wsem = "xw0" if sub % 2 == 0 else "xw1"
        rsem = "xw1" if sub % 2 == 0 else "xw0"
        stash_done = [(f"sg{j}", ctr[f"sg{j}"]) for j in range(4)]
        src_done = [] if sub == 0 else [(rsem, ctr[rsem])]
        last = None
        for s in range(NS):
            cA = dma("sync", "eA", ebA, stash_d[s * 128:(s + 1) * 128, :],
                     waits=flat(ev["ebA_free"], stash_done))
            cB = dma("sync", "eB", ebB, src[s * 128:(s + 1) * 128, :],
                     waits=flat(ev["ebB_free"], src_done))
            p1 = emit("gpsimd", lambda e: e.tensor_tensor(out=ebA, in0=ebA, in1=ebB, op=ALU.add),
                      inc="pS", waits=[("eA", cA), ("eB", cB)])
            ev["ebB_free"] = [("pS", p1)]
            cw = dma("gpsimd", wsem, dst[s * 128:(s + 1) * 128, :], ebA, waits=[("pS", p1)])
            last = cw
            if sub < 3:
                ln_chain(s, [("pS", p1)], ebA)
                ev["ebA_free"] = [("pS", ctr["pS"]), (wsem, cw)]
            else:
                ev["ebA_free"] = [(wsem, cw)]
        if sub < 3:
            ev["xnf"] = [("vS", ctr["vS"])]
        return last

    def ffn():
        for sg in range(4):
            for icb in range(16):
                for kc in range(NCH):
                    load_w_i8(kc, wg8_d[kc * 128:(kc + 1) * 128, icb * 512:(icb + 1) * 512])
                wrdy = flat(*[wbf_ready[x] for x in range(NCH)])
                gate_ps = []
                for j in range(4):
                    ic = icb * 4 + j
                    tl = None
                    for kc in range(NCH):
                        tl = emit("tensor", lambda e, kc_=kc, j_=j, ic_=ic, sg_=sg: e.matmul(
                            out=scor[:, j_ * 512:(j_ + 1) * 512],
                            lhsT=wbf[:, kc_ * 512 + (ic_ % 4) * 128: kc_ * 512 + (ic_ % 4) * 128 + 128],
                            rhs=xnf[:, kc_ * S + sg_ * 512: kc_ * S + (sg_ + 1) * 512],
                            start=(kc_ == 0), stop=(kc_ == NCH - 1)),
                            inc="tS" if kc == NCH - 1 else None,
                            waits=flat(wrdy, scor_free[0], ev["xnf"]) if (kc == 0 and j == 0) else ())
                    gate_ps.append(tl)
                for kc in range(NCH):
                    wbf_free[kc] = [("tS", ctr["tS"])]
                silu_ev = []
                for j in range(4):
                    ic = icb * 4 + j
                    aa = emit("scalar", lambda e, j_=j, ic_=ic: e.activation(
                        out=ebB[:, j_ * 512:(j_ + 1) * 512], in_=scor[:, j_ * 512:(j_ + 1) * 512],
                        func=AF.Silu, scale=sgc[:, ic_:ic_ + 1]),
                        inc="aS", waits=flat([("tS", gate_ps[j])], ev["ebB_free"] if j == 0 else []))
                    silu_ev.append(aa)
                scor_free[0] = [("aS", silu_ev[-1])]
                for kc in range(NCH):
                    load_w_i8(kc, wu8_d[kc * 128:(kc + 1) * 128, icb * 512:(icb + 1) * 512])
                wrdy = flat(*[wbf_ready[x] for x in range(NCH)])
                up_ps = []
                for j in range(4):
                    ic = icb * 4 + j
                    tl = None
                    for kc in range(NCH):
                        tl = emit("tensor", lambda e, kc_=kc, j_=j, ic_=ic, sg_=sg: e.matmul(
                            out=scor[:, j_ * 512:(j_ + 1) * 512],
                            lhsT=wbf[:, kc_ * 512 + (ic_ % 4) * 128: kc_ * 512 + (ic_ % 4) * 128 + 128],
                            rhs=xnf[:, kc_ * S + sg_ * 512: kc_ * S + (sg_ + 1) * 512],
                            start=(kc_ == 0), stop=(kc_ == NCH - 1)),
                            inc="tS" if kc == NCH - 1 else None,
                            waits=flat(wrdy, scor_free[0]) if (kc == 0 and j == 0) else ())
                    up_ps.append(tl)
                for kc in range(NCH):
                    wbf_free[kc] = [("tS", ctr["tS"])]
                hv = []
                for j in range(4):
                    ic = icb * 4 + j
                    vv = emit("vector", lambda e, j_=j, ic_=ic: e.tensor_scalar(
                        out=ebA[:, j_ * 512:(j_ + 1) * 512], in0=scor[:, j_ * 512:(j_ + 1) * 512],
                        scalar1=suc[:, ic_:ic_ + 1], scalar2=None, op0=ALU.mult),
                        inc="vS", waits=flat([("tS", up_ps[j])], ev["ebA_free"] if j == 0 else []))
                    pp = emit("gpsimd", lambda e, j_=j, ic_=ic: e.tensor_tensor(
                        out=arena[:, F_HT + ic_ * 512: F_HT + (ic_ + 1) * 512],
                        in0=ebB[:, j_ * 512:(j_ + 1) * 512], in1=ebA[:, j_ * 512:(j_ + 1) * 512], op=ALU.mult),
                        inc="pS", waits=[("aS", silu_ev[j]), ("vS", vv)])
                    hv.append(pp)
                scor_free[0] = [("vS", ctr["vS"])]
                ev["ebB_free"] = [("pS", hv[-1])]
                ev["ebA_free"] = [("pS", hv[-1])]
            ev["hT"] = [("pS", ctr["pS"])]
            cA = dma("sync", "eA", ebA, sdbc_d[:, :], waits=flat(ev["ebA_free"]))
            ev["ebA_bias"] = [("eA", cA)]
            sb = [sg * 4 + j for j in range(4)]
            for ot in range(4):
                for kb in range(0, NIC, 16):
                    nblk = 16
                    for k in range(nblk):
                        load_w_i8(k, wd8_d[(kb + k) * 128:(kb + k + 1) * 128, ot * 512:(ot + 1) * 512])
                    wrdy = flat(*[wbf_ready[x] for x in range(nblk)])
                    for j, s_g in enumerate(sb):
                        for k in range(nblk):
                            kk2 = kb + k
                            w_ = flat(wrdy, ev["hT"],
                                      scor_free[0] if kb == 0 else []) if (k == 0 and j == 0) else ()
                            emit("tensor", lambda e, j_=j, k_=k, kk_=kk2, jj=j: e.matmul(
                                out=scor[:, j_ * 512:(j_ + 1) * 512],
                                lhsT=arena[:, F_HT + kk_ * 512 + jj * 128: F_HT + kk_ * 512 + jj * 128 + 128],
                                rhs=wbf[:, k_ * 512:(k_ + 1) * 512],
                                start=(kk_ == 0), stop=(kk_ == NIC - 1)),
                                inc="tS" if (k == nblk - 1 and j == 3) else None,
                                waits=w_)
                    for x in range(nblk):
                        wbf_free[x] = [("tS", ctr["tS"])]
                tl = ctr["tS"]
                for j, s_g in enumerate(sb):
                    evac_stash(s_g, ot, scor[:, j * 512:(j + 1) * 512], "sdown",
                               flat([("tS", tl)], ev["ebA_bias"]))
                scor_free[0] = [("vS", ctr["vS"])]
            ev["ebA_free"] = [("vS", ctr["vS"])]
            ev["ebB_free"] = [(f"sg{j}", ctr[f"sg{j}"]) for j in range(4)]

    def preconvert_wout():
        pf_free = [[], []]
        for k in range(64):
            for j in range(4):
                jj = j % 2
                c = dma("scalar", f"pf{jj}", pcf[:, jj * 512:(jj + 1) * 512],
                        wout_d[k * 128:(k + 1) * 128, j * 512:(j + 1) * 512],
                        waits=flat(pf_free[jj]))
                aa = emit("scalar", lambda e, jj_=jj: e.copy(out=pcv[:, 0:512],
                                                             in_=pcf[:, jj_ * 512:(jj_ + 1) * 512]),
                          inc="aS", waits=flat([(f"pf{jj}", c)], pcv_free[0]))
                pf_free[jj] = [("aS", aa)]
                cw = dma("gpsimd", "pc", woutb_d[k, :, j * 512:(j + 1) * 512], pcv[:, 0:512],
                         waits=[("aS", aa)])
                pcv_free[0] = [("pc", cw)]
        return [("pc", ctr["pc"])]

    def attn_b():
        ev["m1_free"] = flat(ev["ebA_free"])
        ev["m2_free"] = flat(ev["ebA_free"])
        woutb_ready = preconvert_wout()
        for kk in range(4):
            qkv_v = []
            for kc in range(NCH):
                load_w_f32(kc, wattn_d[kc * 128:(kc + 1) * 128, kk * 512:(kk + 1) * 512])
            wrdy = flat(*[wbf_ready[x] for x in range(NCH)])
            for hh in range(4):
                for stt in range(NT):
                    pp = next_psum()
                    tl = proj_accum(pp,
                                    [wbf[:, kc * 512 + hh * 128: kc * 512 + hh * 128 + 128] for kc in range(NCH)],
                                    [xnf[:, kc * S + stt * 512: kc * S + (stt + 1) * 512] for kc in range(NCH)],
                                    extra_waits=flat(wrdy, ev["xnf"]))
                    fe, re = rope_evac(pp, stt, arena[:, B_Q4 + hh * S + stt * 512: B_Q4 + hh * S + (stt + 1) * 512], tl)
                    psum_free[id(pp)] = fe
                    qkv_v.append(re[0][1])
            for kc in range(NCH):
                wbf_free[kc] = [("tS", ctr["tS"])]
            for kc in range(NCH):
                load_w_f32(kc, wattn_d[kc * 128:(kc + 1) * 128, 2048 + kk * 128: 2048 + (kk + 1) * 128], w=128)
            wrdy = flat(*[wbf_ready[x] for x in range(NCH)])
            for stt in range(NT):
                pp = next_psum()
                tl = proj_accum(pp,
                                [wbf[:, kc * 512: kc * 512 + 128] for kc in range(NCH)],
                                [xnf[:, kc * S + stt * 512: kc * S + (stt + 1) * 512] for kc in range(NCH)],
                                extra_waits=wrdy)
                fe, re = rope_evac(pp, stt, arena[:, B_KB + stt * 512: B_KB + (stt + 1) * 512], tl)
                psum_free[id(pp)] = fe
                qkv_v.append(re[0][1])
            for kc in range(NCH):
                wbf_free[kc] = [("tS", ctr["tS"])]
            for kc in range(NCH):
                load_w_f32(kc, wattn_d[kc * 128:(kc + 1) * 128, 2560 + kk * 512: 2560 + (kk + 1) * 512])
            wrdy = flat(*[wbf_ready[x] for x in range(NCH)])
            for ssub in range(NS):
                pp = next_psum()
                tl = proj_accum(pp,
                                [xnf[:, kc * S + ssub * 128: kc * S + ssub * 128 + 128] for kc in range(NCH)],
                                [wbf[:, kc * 512:(kc + 1) * 512] for kc in range(NCH)],
                                extra_waits=wrdy)
                vv = emit("vector", lambda e, pp_=pp, s_=ssub: e.tensor_copy(
                    out=arena[:, B_V4 + s_ * 512: B_V4 + (s_ + 1) * 512], in_=pp_[:, 0:512]),
                    inc="vS", waits=[("tS", tl)])
                psum_free[id(pp)] = [("vS", vv)]
                qkv_v.append(vv)
            for kc in range(NCH):
                wbf_free[kc] = [("tS", ctr["tS"])]
            ev["qkv"] = [("vS", max(qkv_v))]
            for hh in range(4):
                h = 4 * kk + hh
                for J in range(4):
                    for p in range(4):
                        softmax_transpose(4, J, p, False, B_Q4 + hh * S, B_KB, B_PT, B_PR)
                    for dvc in range(4):
                        pp = next_psum()
                        tl = proj_accum(pp,
                                        [arena[:, B_V4 + kc * 512 + dvc * 128: B_V4 + kc * 512 + dvc * 128 + 128] for kc in range(16)],
                                        [arena[:, B_PT + kc * 512: B_PT + (kc + 1) * 512] for kc in range(16)],
                                        extra_waits=flat(probs_free[0], ev["ctxs_free"]))
                        vv = emit("vector", lambda e, pp_=pp, J_=J, d_=dvc: e.tensor_copy(
                            out=arena[:, B_CS + d_ * 2048 + J_ * 512: B_CS + d_ * 2048 + (J_ + 1) * 512],
                            in_=pp_[:, 0:512]),
                            inc="vS", waits=[("tS", tl)])
                        psum_free[id(pp)] = [("vS", vv)]
                cw = None
                for dvc in range(4):
                    cw = dma("gpsimd", "cw", ctxb_d[h * 4 + dvc, :, :],
                             arena[:, B_CS + dvc * 2048: B_CS + (dvc + 1) * 2048],
                             waits=[("vS", ctr["vS"])])
                ev["ctxs_free"] = [("cw", cw)]
        ev["ebA_free"] = flat(ev["m1_free"], ev["m2_free"])
        return woutb_ready

    attn_a()
    ctxa_done = [("cw", ctr["cw"])]

    def mov_woa(slot, k, ot):
        load_w_f32(slot, wot_d[k * 128:(k + 1) * 128, ot * 512:(ot + 1) * 512])
    run_outproj(16, "plain", mov_woa, ctxa_d, ctxa_done, None)
    epilogue(0)
    ffn()
    epilogue(1)
    woutb_ready = attn_b()
    ctxb_done = [("cw", ctr["cw"])]

    def mov_wob(slot, k, ot):
        load_w_bf16(slot, woutb_d[k, :, ot * 512:(ot + 1) * 512], extra=woutb_ready)
    run_outproj(64, "bias", mov_wob, ctxb_d, ctxb_done, bbc_d)
    epilogue(2)
    ffn()
    epilogue(3)

    emit("gpsimd", lambda e: e.memset(cols[:, 15:16], 0.0),
         waits=[(k, ctr[k]) for k in SEM_NAMES if ctr[k] > 0])

    with nc.Block() as block:
        def mk(ename):
            def sec(eng):
                for ws, fn, inc, n in streams[ename]:
                    for (s_, v) in ws:
                        eng.wait_ge(sems[s_], v)
                    inst = fn(eng)
                    if inc is not None:
                        inst.then_inc(sems[inc], n)
            sec.__name__ = ename
            return sec
        block.tensor(mk("tensor"))
        block.vector(mk("vector"))
        block.scalar(mk("scalar"))
        block.gpsimd(mk("gpsimd"))
        block.sync(mk("sync"))
    est.close()
    n_inst = sum(len(v) for v in streams.values())
    return nc, n_inst


def _host_constants():
    HD = 128
    TH = 10000.0
    inv = 1.0 / TH ** (np.arange(0, HD, 2, dtype=np.float64) / HD)
    t = np.arange(S, dtype=np.float64)
    ang_half = np.outer(inv, t)
    ang = np.concatenate([ang_half, ang_half], 0)
    cosb = np.cos(ang).astype(ml_dtypes.bfloat16)
    sin = np.sin(ang)
    sinS = np.concatenate([-sin[:64], sin[64:]], 0).astype(ml_dtypes.bfloat16)
    qq = np.arange(128)[:, None]
    m = np.zeros((128, 2048), dtype=np.float32)
    for p in range(4):
        kkk = np.arange(512)[None, :]
        m[:, p * 512:(p + 1) * 512] = (kkk <= p * 128 + qq).astype(np.float32)
    mask01 = m.astype(ml_dtypes.bfloat16)
    ident = np.eye(128, dtype=np.float32).astype(ml_dtypes.bfloat16)
    return cosb, sinS, mask01, ident


_CACHE = {}


def kernel(L0_attn_a_hidden_states, L0_attn_a_Wq, L0_attn_a_Wk, L0_attn_a_Wv, L0_attn_a_Wo,
           L0_attn_b_W_attn, L0_attn_b_W_out, L0_attn_b_b_out,
           L0_ffn_w_gate, L0_ffn_w_up, L0_ffn_w_down,
           L0_ffn_s_gate, L0_ffn_s_up, L0_ffn_s_down):
    x = np.asarray(L0_attn_a_hidden_states, dtype=np.float32)
    if "nc" not in _CACHE:
        _CACHE["nc"], _CACHE["n"] = build_program()
    nc = _CACHE["nc"]
    cosb, sinS, mask01, ident = _host_constants()
    wqt = np.ascontiguousarray(np.asarray(L0_attn_a_Wq, np.float32).T)
    wkt = np.ascontiguousarray(np.asarray(L0_attn_a_Wk, np.float32).T)
    wvt = np.ascontiguousarray(np.asarray(L0_attn_a_Wv, np.float32).T)
    wot = np.ascontiguousarray(np.asarray(L0_attn_a_Wo, np.float32).T)
    wattn = np.ascontiguousarray(np.asarray(L0_attn_b_W_attn, np.float32))
    wout = np.ascontiguousarray(np.asarray(L0_attn_b_W_out, np.float32))
    wg8 = np.ascontiguousarray(np.asarray(L0_ffn_w_gate, np.int8))
    wu8 = np.ascontiguousarray(np.asarray(L0_ffn_w_up, np.int8))
    wd8 = np.ascontiguousarray(np.asarray(L0_ffn_w_down, np.int8))
    sgc = np.ascontiguousarray(np.asarray(L0_ffn_s_gate, np.float32).reshape(64, 128).T)
    suc = np.ascontiguousarray(np.asarray(L0_ffn_s_up, np.float32).reshape(64, 128).T)
    sdbc = np.ascontiguousarray(np.broadcast_to(np.asarray(L0_ffn_s_down, np.float32)[None, :], (128, D)))
    bbc = np.ascontiguousarray(np.broadcast_to(np.asarray(L0_attn_b_b_out, np.float32)[None, :], (128, D)))
    shared = dict(wqt=wqt, wkt=wkt, wvt=wvt, wot=wot, wattn=wattn, wout=wout,
                  wg8=wg8, wu8=wu8, wd8=wd8, sgc=sgc, suc=suc, sdbc=sdbc, bbc=bbc,
                  cosb=cosb, sinb=sinS, mask01=mask01, ident=ident)
    in_maps = []
    for c in range(8):
        m = dict(shared)
        m["x"] = np.ascontiguousarray(x[c])
        in_maps.append(m)
    res = run_bass_kernel_spmd(nc, in_maps, core_ids=list(range(8)))
    return np.stack([np.asarray(r["out"]) for r in res.results], axis=0)


if __name__ == "__main__":
    nc, n = build_program()
    print("built", n, "instructions")


# version 03
# speedup vs baseline: 1.8062x; 1.8062x over previous
import math
import numpy as np
import ml_dtypes
from contextlib import ExitStack
import concourse.bass as bass
import concourse.mybir as mybir
from concourse.bass_utils import run_bass_kernel_spmd

F32 = mybir.dt.float32
BF16 = mybir.dt.bfloat16
I8 = mybir.dt.int8
AF = mybir.ActivationFunctionType
ALU = mybir.AluOpType
AX = mybir.AxisListType

S = 2048
D = 2048
NCH = 16
NS = 16
NT = 4
INTER = 8192
NIC = 64
SCALE = 1.0 / math.sqrt(128.0)
EPS = 1e-5

ARENA_COLS = 36864
A_Q4, A_K4, A_V4, A_PT, A_PR, A_CS = 0, 8192, 16384, 24576, 32768, 34816
B_Q4, B_KB, B_V4, B_PT, B_PR, B_CS = 0, 8192, 10240, 18432, 26624, 28672
F_HT = 0
XN_TM = 32768
OP_CT = 0


def build_program():
    nc = bass.Bass("TRN2", target_bir_lowering=False)

    def din(name, shape, dt):
        return nc.dram_tensor(name, shape, dt, kind="ExternalInput").ap()

    x_d = din("x", [S, D], F32)
    wqt_d = din("wqt", [D, D], F32)
    wkt_d = din("wkt", [D, D], F32)
    wvt_d = din("wvt", [D, D], F32)
    wot_d = din("wot", [D, D], F32)
    wattn_d = din("wattn", [D, 4608], F32)
    wout_d = din("wout", [INTER, D], F32)
    wg8_d = din("wg8", [D, INTER], I8)
    wu8_d = din("wu8", [D, INTER], I8)
    wd8_d = din("wd8", [INTER, D], I8)
    sgc_d = din("sgc", [128, 64], F32)
    suc_d = din("suc", [128, 64], F32)
    sdbc_d = din("sdbc", [128, D], F32)
    bbc_d = din("bbc", [128, D], F32)
    cosb_d = din("cosb", [128, S], BF16)
    sinb_d = din("sinb", [128, S], BF16)
    mask_d = din("mask01", [128, D], BF16)
    id_d = din("ident", [128, 128], BF16)

    out_d = nc.dram_tensor("out", [S, D], F32, kind="ExternalOutput").ap()
    xres_d = nc.dram_tensor("xres", [S, D], F32, kind="Internal").ap()
    stash_d = nc.dram_tensor("stash", [S, D], F32, kind="Internal").ap()
    ctxa_d = nc.dram_tensor("ctxa", [16, 128, S], BF16, kind="Internal").ap()
    ctxb_d = nc.dram_tensor("ctxb", [64, 128, S], BF16, kind="Internal").ap()
    wqtb_d = nc.dram_tensor("wqtb", [16, 128, D], BF16, kind="Internal").ap()
    wktb_d = nc.dram_tensor("wktb", [16, 128, D], BF16, kind="Internal").ap()
    wvtb_d = nc.dram_tensor("wvtb", [16, 128, D], BF16, kind="Internal").ap()
    wotb_d = nc.dram_tensor("wotb", [16, 128, D], BF16, kind="Internal").ap()
    wattnb_d = nc.dram_tensor("wattnb", [16, 128, 4608], BF16, kind="Internal").ap()
    woutb_d = nc.dram_tensor("woutb", [64, 128, D], BF16, kind="Internal").ap()
    wgb_d = nc.dram_tensor("wgb", [16, 128, INTER], BF16, kind="Internal").ap()
    wub_d = nc.dram_tensor("wub", [16, 128, INTER], BF16, kind="Internal").ap()
    wdb_d = nc.dram_tensor("wdb", [64, 128, D], BF16, kind="Internal").ap()

    est = ExitStack()
    sems = {}
    SEM_NAMES = (["cN", "eA", "eB", "wb", "rp", "cw", "xw0", "xw1"]
                 + [f"pf{j}" for j in range(2)]
                 + [f"sg{j}" for j in range(4)]
                 + [f"wi{j}" for j in range(8)]
                 + [f"r{j}" for j in range(8)]
                 + [f"so{j}" for j in range(8)]
                 + [f"pc{j}" for j in range(4)]
                 + ["tS", "vS", "aS", "pS"])
    for k in SEM_NAMES:
        sems[k] = est.enter_context(nc.semaphore(k))
    DMA_SEMS = set(SEM_NAMES) - {"tS", "vS", "aS", "pS"}

    xnf = nc.alloc_sbuf_tensor("xnf", [128, NCH * S], BF16).ap()
    cosb = nc.alloc_sbuf_tensor("sb_cosb", [128, S], BF16).ap()
    sinb = nc.alloc_sbuf_tensor("sb_sinb", [128, S], BF16).ap()
    ident = nc.alloc_sbuf_tensor("sb_ident", [128, 128], BF16).ap()
    mask01 = nc.alloc_sbuf_tensor("sb_mask01", [128, D], BF16).ap()
    sgc = nc.alloc_sbuf_tensor("sb_sgc", [128, 64], F32).ap()
    suc = nc.alloc_sbuf_tensor("sb_suc", [128, 64], F32).ap()
    cols = nc.alloc_sbuf_tensor("cols", [128, 16], F32).ap()
    arena = nc.alloc_sbuf_tensor("arena", [128, ARENA_COLS], BF16).ap()
    ebA = nc.alloc_sbuf_tensor("ebA", [128, D], F32).ap()
    ebB = nc.alloc_sbuf_tensor("ebB", [128, D], F32).ap()
    ropeb = nc.alloc_sbuf_tensor("ropeb", [128, 1024], BF16).ap()
    wbf = nc.alloc_sbuf_tensor("wbf", [128, 16 * 512], BF16).ap()
    wsti = nc.alloc_sbuf_tensor("wsti", [128, 8 * 512], I8).ap()
    pcv = nc.alloc_sbuf_tensor("pcv", [128, 4 * 512], BF16).ap()
    pcf = nc.alloc_sbuf_tensor("pcf", [128, 2 * 512], F32).ap()

    scor = nc.alloc_psum_tensor("scor", [128, 2048], F32).ap()
    pA = nc.alloc_psum_tensor("pA", [128, 512], F32).ap()
    pB = nc.alloc_psum_tensor("pB", [128, 512], F32).ap()
    pC = nc.alloc_psum_tensor("pC", [128, 512], F32).ap()
    ptp = nc.alloc_psum_tensor("ptp", [128, 512], BF16).ap()

    streams = {e: [] for e in ("tensor", "vector", "scalar", "gpsimd", "sync")}
    ctr = {k: 0 for k in sems}
    water = {e: {k: 0 for k in sems} for e in streams}

    def flat(*evs):
        out = []
        for e in evs:
            if e is None:
                continue
            if isinstance(e, list):
                out.extend(x for x in e if x is not None)
            else:
                out.append(e)
        return out

    def emit(eng, fn, inc=None, waits=()):
        byname = {}
        for (s_, v) in waits:
            if v > byname.get(s_, 0):
                byname[s_] = v
        ws = []
        for s_, v in byname.items():
            if v > water[eng][s_]:
                ws.append((s_, v))
                water[eng][s_] = v
        n = 0
        if inc is not None:
            n = 16 if inc in DMA_SEMS else 1
            ctr[inc] += n
        streams[eng].append((ws, fn, inc, n))
        return ctr[inc] if inc else None

    def dma(q, sem, out, in_, waits=()):
        return emit(q, lambda e, o=out, i=in_: e.dma_start(out=o, in_=i), inc=sem, waits=waits)

    ev = {
        "xnf": [], "ebA_free": [], "ebB_free": [], "ptp_free": [], "ln_free": [],
        "ropeb_free": [], "m1_free": [], "m2_free": [], "qkv": [], "ctxs_free": [],
        "consts": [], "hT": [], "ebA_bias": [],
    }

    pcw = {}
    _pcf_free = [[], []]
    _pcf_next = [0]
    _wsti_free = [[] for _ in range(8)]
    _wsti_next = [0]
    _pcv_free = [[] for _ in range(4)]
    _pcv_next = [0]

    def _pcv_slot():
        j = _pcv_next[0]; _pcv_next[0] = (j + 1) % 4
        return j

    def preconv_f32(src_tile, dst_tile):
        j = _pcf_next[0]; _pcf_next[0] = (j + 1) % 2
        c = dma("scalar", f"pf{j}", pcf[:, j * 512:(j + 1) * 512], src_tile, waits=_pcf_free[j])
        v = _pcv_slot()
        aa = emit("scalar", lambda e, jj=j, vv=v: e.copy(out=pcv[:, vv * 512:(vv + 1) * 512],
                                                         in_=pcf[:, jj * 512:(jj + 1) * 512]),
                  inc="aS", waits=flat([(f"pf{j}", c)], _pcv_free[v]))
        _pcf_free[j] = [("aS", aa)]
        cw = dma("gpsimd", f"pc{v}", dst_tile, pcv[:, v * 512:(v + 1) * 512], waits=[("aS", aa)])
        _pcv_free[v] = [(f"pc{v}", cw)]

    def preconv_i8(src_tile, dst_tile):
        j = _wsti_next[0]; _wsti_next[0] = (j + 1) % 8
        c = dma("scalar", f"wi{j}", wsti[:, j * 512:(j + 1) * 512], src_tile, waits=_wsti_free[j])
        v = _pcv_slot()
        pp = emit("gpsimd", lambda e, jj=j, vv=v: e.tensor_copy(out=pcv[:, vv * 512:(vv + 1) * 512],
                                                                in_=wsti[:, jj * 512:(jj + 1) * 512]),
                  inc="pS", waits=flat([(f"wi{j}", c)], _pcv_free[v]))
        _wsti_free[j] = [("pS", pp)]
        cw = dma("gpsimd", f"pc{v}", dst_tile, pcv[:, v * 512:(v + 1) * 512], waits=[("pS", pp)])
        _pcv_free[v] = [(f"pc{v}", cw)]

    def pcw_mark(name):
        pcw[name] = [(f"pc{j}", ctr[f"pc{j}"]) for j in range(4)]

    def emit_preconvert_qkv_weights():
        for g in range(4):
            for (srcd, dstd) in ((wqt_d, wqtb_d), (wkt_d, wktb_d), (wvt_d, wvtb_d)):
                for kc in range(NCH):
                    preconv_f32(srcd[kc * 128:(kc + 1) * 128, g * 512:(g + 1) * 512],
                                dstd[kc, :, g * 512:(g + 1) * 512])
            pcw_mark(f"qkv{g}")

    def emit_preconvert_rest():
        for kc in range(NCH):
            for j in range(4):
                preconv_f32(wot_d[kc * 128:(kc + 1) * 128, j * 512:(j + 1) * 512],
                            wotb_d[kc, :, j * 512:(j + 1) * 512])
        pcw_mark("wot")
        for kc in range(NCH):
            for j in range(16):
                preconv_i8(wg8_d[kc * 128:(kc + 1) * 128, j * 512:(j + 1) * 512],
                           wgb_d[kc, :, j * 512:(j + 1) * 512])
                preconv_i8(wu8_d[kc * 128:(kc + 1) * 128, j * 512:(j + 1) * 512],
                           wub_d[kc, :, j * 512:(j + 1) * 512])
        for k in range(NIC):
            for j in range(4):
                preconv_i8(wd8_d[k * 128:(k + 1) * 128, j * 512:(j + 1) * 512],
                           wdb_d[k, :, j * 512:(j + 1) * 512])
        pcw_mark("ffn")
        for kc in range(NCH):
            for j in range(9):
                preconv_f32(wattn_d[kc * 128:(kc + 1) * 128, j * 512:(j + 1) * 512],
                            wattnb_d[kc, :, j * 512:(j + 1) * 512])
        pcw_mark("wattn")
        for k in range(NIC):
            for j in range(4):
                preconv_f32(wout_d[k * 128:(k + 1) * 128, j * 512:(j + 1) * 512],
                            woutb_d[k, :, j * 512:(j + 1) * 512])
        pcw_mark("wout")

    wbf_free = [[] for _ in range(16)]
    wbf_ready = [[] for _ in range(16)]

    def load_blk(slot, dram_tile, w=512, extra=()):
        c = dma("sync", "wb", wbf[:, slot * 512:slot * 512 + w], dram_tile,
                waits=flat(wbf_free[slot], list(extra)))
        wbf_ready[slot] = [("wb", c)]

    ring_free = [[] for _ in range(8)]

    def ring_load(k, dram_tile, extra=()):
        j = k % 8
        c = dma("sync", f"r{j}", wbf[:, j * 512:(j + 1) * 512], dram_tile,
                waits=flat(ring_free[j], list(extra)))
        return j, [(f"r{j}", c)]

    so_free = [[] for _ in range(8)]

    def fence_wbf():
        e_ = [("tS", ctr["tS"])]
        for j in range(16):
            wbf_free[j] = list(e_)
        for j in range(8):
            ring_free[j] = list(e_)
            so_free[j] = list(e_)

    def stat_ring_load(k, dram_tile, extra=()):
        j = k % 8
        c = dma("sync", f"so{j}", arena[:, OP_CT + j * 512: OP_CT + (j + 1) * 512], dram_tile,
                waits=flat(so_free[j], list(extra)))
        return j, [(f"so{j}", c)]

    dma("sync", "cN", cosb, cosb_d[:, :])
    dma("sync", "cN", sinb, sinb_d[:, :])
    dma("sync", "cN", ident, id_d[:, :])
    dma("sync", "cN", mask01, mask_d[:, :])
    dma("sync", "cN", sgc, sgc_d[:, :])
    c = dma("sync", "cN", suc, suc_d[:, :])
    ev["consts"] = [("cN", c)]

    def ln_chain(s, pre_waits, buf):
        pw = flat(pre_waits, ev["ln_free"])
        v1 = emit("vector", lambda e, b=buf: e.reduce_sum(out=cols[:, 0:1], in_=b, axis=AX.X),
                  inc="vS", waits=pw)
        a1 = emit("scalar", lambda e, b=buf: e.activation(
            out=arena[:, XN_TM:XN_TM + 2048], in_=b, func=AF.Square, accum_out=cols[:, 1:2]),
            inc="aS", waits=pw)
        a2 = emit("scalar", lambda e: e.mul(out=cols[:, 2:3], in_=cols[:, 0:1], mul=1.0 / D),
                  inc="aS", waits=[("vS", v1)])
        v2 = emit("vector", lambda e: e.tensor_scalar(out=cols[:, 3:4], in0=cols[:, 2:3],
                                                      scalar1=cols[:, 2:3], scalar2=-EPS,
                                                      op0=ALU.mult, op1=ALU.add),
                  inc="vS", waits=[("aS", a2)])
        p1 = emit("gpsimd", lambda e: e.tensor_scalar(out=cols[:, 4:5], in0=cols[:, 1:2],
                                                      scalar1=1.0 / D, scalar2=cols[:, 3:4],
                                                      op0=ALU.mult, op1=ALU.subtract),
                  inc="pS", waits=[("aS", a1), ("vS", v2)])
        a3 = emit("scalar", lambda e: e.activation(out=cols[:, 5:6], in_=cols[:, 4:5],
                                                   func=AF.Sqrt),
                  inc="aS", waits=[("pS", p1)])
        v3 = emit("vector", lambda e: e.reciprocal(out=cols[:, 6:7], in_=cols[:, 5:6]),
                  inc="vS", waits=[("aS", a3)])
        p2 = emit("gpsimd", lambda e, b=buf: e.tensor_scalar(
            out=arena[:, XN_TM:XN_TM + 2048], in0=b,
            scalar1=cols[:, 2:3], scalar2=cols[:, 6:7], op0=ALU.subtract, op1=ALU.mult),
            inc="pS", waits=[("vS", v3), ("aS", a1)])
        last_t = 0
        xnfv = xnf.rearrange("a (b c) -> a b c", b=NCH)
        last_v = None
        for g4 in range(4):
            t = None
            for j in range(4):
                kc = g4 * 4 + j
                t = emit("tensor", lambda e, kk=kc, jj=j: e.transpose(
                    out=ptp[:, jj * 128:(jj + 1) * 128],
                    in_=arena[:, XN_TM + kk * 128:XN_TM + (kk + 1) * 128], identity=ident),
                    inc="tS" if j == 3 else None,
                    waits=flat([("pS", p2)], ev["ptp_free"], ev["consts"]) if j == 0 else ())
            last_t = t
            src = ptp.rearrange("a (b c) -> a b c", b=4)
            dst = xnfv[:, g4 * 4:g4 * 4 + 4, s * 128:(s + 1) * 128]
            last_v = emit("vector", lambda e, sr=src, dt_=dst: e.tensor_copy(out=dt_, in_=sr),
                          inc="vS", waits=[("tS", t)])
            ev["ptp_free"] = [("vS", last_v)]
        ev["ln_free"] = [("pS", p2), ("tS", last_t), ("vS", last_v)]
        return last_v

    emit_preconvert_qkv_weights()
    xnf_last = 0
    for s in range(NS):
        buf, key, sm = (ebA, "ebA_free", "eA") if s % 2 == 0 else (ebB, "ebB_free", "eB")
        c = dma("sync", sm, buf, x_d[s * 128:(s + 1) * 128, :], waits=flat(ev[key], ev["consts"]))
        xnf_last = ln_chain(s, [(sm, c)], buf)
        ev[key] = [("pS", ctr["pS"])]
    ev["xnf"] = [("vS", xnf_last)]
    emit_preconvert_rest()

    psum_rot = [pA, pB, pC]
    psum_free = {id(pA): [], id(pB): [], id(pC): []}
    rot_i = [0]

    def next_psum():
        p = psum_rot[rot_i[0] % 3]
        rot_i[0] += 1
        return p

    def proj_accum(dst_psum, stat_slices, rhs_slices, extra_waits=()):
        n = len(stat_slices)
        tlast = None
        for k in range(n):
            tlast = emit("tensor", lambda e, a=stat_slices[k], b=rhs_slices[k], k_=k, n_=n,
                         dp=dst_psum: e.matmul(out=dp[:, 0:512], lhsT=a, rhs=b,
                                               start=(k_ == 0), stop=(k_ == n_ - 1)),
                         inc="tS" if k == n - 1 else None,
                         waits=flat(list(extra_waits), psum_free[id(dst_psum)]) if k == 0 else ())
        return tlast

    def rope_evac(psum, stt, dst_bf, mm_done):
        a1 = emit("scalar", lambda e, p=psum: e.copy(out=ropeb[:, 0:512], in_=p[:, 0:512]),
                  inc="aS", waits=flat([("tS", mm_done)], ev["ropeb_free"]))
        c1 = dma("gpsimd", "rp", ropeb[0:64, 512:1024], ropeb[64:128, 0:512], waits=[("aS", a1)])
        c2 = dma("gpsimd", "rp", ropeb[64:128, 512:1024], ropeb[0:64, 0:512], waits=[("aS", a1)])
        v1 = emit("vector", lambda e, s_=stt: e.tensor_tensor(
            out=ebA[:, 0:512], in0=ropeb[:, 0:512], in1=cosb[:, s_ * 512:(s_ + 1) * 512], op=ALU.mult),
            inc="vS", waits=flat([("aS", a1)], ev["m1_free"]))
        p1 = emit("gpsimd", lambda e, s_=stt: e.tensor_tensor(
            out=ebA[:, 512:1024], in0=ropeb[:, 512:1024], in1=sinb[:, s_ * 512:(s_ + 1) * 512], op=ALU.mult),
            inc="pS", waits=flat([("rp", c2)], ev["m2_free"]))
        v2 = emit("vector", lambda e, d=dst_bf: e.tensor_tensor(
            out=d, in0=ebA[:, 0:512], in1=ebA[:, 512:1024], op=ALU.add),
            inc="vS", waits=[("vS", v1), ("pS", p1)])
        ev["ropeb_free"] = [("vS", v2)]
        ev["m1_free"] = [("vS", v2)]
        ev["m2_free"] = [("vS", v2)]
        return [("aS", a1)], [("vS", v2)]

    probs_free = [[]]
    scor_free = [[]]

    def softmax_transpose(nkt, J, p, is_a, qoff, koff, pt_off, pr_off):
        i = 4 * J + p
        qsl = arena[:, qoff + i * 128: qoff + (i + 1) * 128]
        tlast = None
        for kt in range(nkt):
            tlast = emit("tensor", lambda e, q=qsl, kt_=kt, ko=koff: e.matmul(
                out=scor[:, kt_ * 512:(kt_ + 1) * 512], lhsT=q,
                rhs=arena[:, ko + kt_ * 512: ko + (kt_ + 1) * 512], start=True, stop=True),
                inc="tS" if kt == nkt - 1 else None,
                waits=flat(scor_free[0], ev["qkv"]) if kt == 0 else ())
        L = nkt * 512
        v1 = emit("vector", lambda e, L_=L: e.reduce_max(out=cols[:, 7:8], in_=scor[:, 0:L_], axis=AX.X),
                  inc="vS", waits=[("tS", tlast)])
        v2 = emit("vector", lambda e: e.tensor_scalar(out=cols[:, 8:9], in0=cols[:, 7:8],
                                                      scalar1=-SCALE, scalar2=None, op0=ALU.mult),
                  inc="vS", waits=[("vS", v1)])
        a1 = emit("scalar", lambda e, L_=L, po=pr_off: e.activation(
            out=arena[:, po:po + L_], in_=scor[:, 0:L_], func=AF.Exp,
            bias=cols[:, 8:9], scale=SCALE, accum_out=cols[:, 9:10]),
            inc="aS", waits=flat([("vS", v2)], probs_free[0]))
        scor_free[0] = [("aS", a1)]
        if is_a:
            pm = emit("gpsimd", lambda e, J_=J, p_=p, po=pr_off: e.tensor_tensor(
                out=arena[:, po + J_ * 512: po + (J_ + 1) * 512],
                in0=arena[:, po + J_ * 512: po + (J_ + 1) * 512],
                in1=mask01[:, p_ * 512:(p_ + 1) * 512], op=ALU.mult),
                inc="pS", waits=[("aS", a1)])
            v3 = emit("vector", lambda e, L_=L, po=pr_off: e.reduce_sum(
                out=cols[:, 9:10], in_=arena[:, po:po + L_], axis=AX.X),
                inc="vS", waits=[("pS", pm)])
            v4 = emit("vector", lambda e: e.reciprocal(out=cols[:, 10:11], in_=cols[:, 9:10]),
                      inc="vS", waits=[("vS", v3)])
        else:
            v4 = emit("vector", lambda e: e.reciprocal(out=cols[:, 10:11], in_=cols[:, 9:10]),
                      inc="vS", waits=[("aS", a1)])
        pn = emit("vector", lambda e, L_=L, po=pr_off: e.tensor_scalar(
            out=arena[:, po:po + L_], in0=arena[:, po:po + L_],
            scalar1=cols[:, 10:11], scalar2=None, op0=ALU.mult),
            inc="vS", waits=[("vS", v4)])
        tl_v = None
        for g4 in range(nkt):
            t = None
            for j in range(4):
                kc = g4 * 4 + j
                t = emit("tensor", lambda e, kk=kc, jj=j, po=pr_off: e.transpose(
                    out=ptp[:, jj * 128:(jj + 1) * 128],
                    in_=arena[:, po + kk * 128: po + (kk + 1) * 128], identity=ident),
                    inc="tS" if j == 3 else None,
                    waits=flat([("vS", pn)], ev["ptp_free"]) if j == 0 else ())
            src = ptp.rearrange("a (b c) -> a b c", b=4)
            dstbase = pt_off + g4 * 4 * 512 + p * 128
            dst = arena[:, dstbase: dstbase + 2048].rearrange("a (b c) -> a b c", b=4)[:, :, 0:128]
            tl_v = emit("vector", lambda e, sr=src, dt_=dst: e.tensor_copy(out=dt_, in_=sr),
                        inc="vS", waits=[("tS", t)])
            ev["ptp_free"] = [("vS", tl_v)]
        probs_free[0] = [("vS", tl_v)]

    def attn_a():
        ev["m1_free"] = flat(ev["ebA_free"])
        ev["m2_free"] = flat(ev["ebA_free"])
        for g in range(4):
            qkv_v = []
            for (wd_, kind) in ((wqtb_d, "q"), (wktb_d, "k"), (wvtb_d, "v")):
                for kc in range(NCH):
                    load_blk(kc, wd_[kc, :, g * 512:(g + 1) * 512],
                             extra=pcw[f"qkv{g}"] if kc == 0 else ())
                wrdy = flat(*[wbf_ready[kc] for kc in range(NCH)])
                if kind in ("q", "k"):
                    base = A_Q4 if kind == "q" else A_K4
                    for hh in range(4):
                        for stt in range(NT):
                            pp = next_psum()
                            tl = proj_accum(pp,
                                            [wbf[:, kc * 512 + hh * 128: kc * 512 + hh * 128 + 128] for kc in range(NCH)],
                                            [xnf[:, kc * S + stt * 512: kc * S + (stt + 1) * 512] for kc in range(NCH)],
                                            extra_waits=flat(wrdy, ev["xnf"]))
                            fe, re = rope_evac(pp, stt,
                                               arena[:, base + hh * S + stt * 512: base + hh * S + (stt + 1) * 512], tl)
                            psum_free[id(pp)] = fe
                            qkv_v.append(re[0][1])
                else:
                    for ssub in range(NS):
                        pp = next_psum()
                        tl = proj_accum(pp,
                                        [xnf[:, kc * S + ssub * 128: kc * S + ssub * 128 + 128] for kc in range(NCH)],
                                        [wbf[:, kc * 512:(kc + 1) * 512] for kc in range(NCH)],
                                        extra_waits=flat(wrdy, ev["xnf"]))
                        vv = emit("vector", lambda e, pp_=pp, s_=ssub: e.tensor_copy(
                            out=arena[:, A_V4 + s_ * 512:A_V4 + (s_ + 1) * 512], in_=pp_[:, 0:512]),
                            inc="vS", waits=[("tS", tl)])
                        psum_free[id(pp)] = [("vS", vv)]
                        qkv_v.append(vv)
                for kc in range(NCH):
                    wbf_free[kc] = [("tS", ctr["tS"])]
            ev["qkv"] = [("vS", max(qkv_v))]
            for hh in range(4):
                h = 4 * g + hh
                for J in range(4):
                    for p in range(4):
                        softmax_transpose(J + 1, J, p, True, A_Q4 + hh * S, A_K4 + hh * S, A_PT, A_PR)
                    nkc = (J + 1) * 4
                    pp = next_psum()
                    tl = proj_accum(pp,
                                    [arena[:, A_V4 + kc * 512 + hh * 128: A_V4 + kc * 512 + hh * 128 + 128] for kc in range(nkc)],
                                    [arena[:, A_PT + kc * 512: A_PT + (kc + 1) * 512] for kc in range(nkc)],
                                    extra_waits=flat(probs_free[0], ev["ctxs_free"]))
                    vv = emit("vector", lambda e, pp_=pp, J_=J: e.tensor_copy(
                        out=arena[:, A_CS + J_ * 512: A_CS + (J_ + 1) * 512], in_=pp_[:, 0:512]),
                        inc="vS", waits=[("tS", tl)])
                    psum_free[id(pp)] = [("vS", vv)]
                cw = dma("gpsimd", "cw", ctxa_d[h, :, :], arena[:, A_CS:A_CS + 2048],
                         waits=[("vS", ctr["vS"])])
                ev["ctxs_free"] = [("cw", cw)]
        ev["ebA_free"] = flat(ev["m1_free"], ev["m2_free"])

    ev["ctxs_free"] = []

    stage_free = [[] for _ in range(4)]

    def evac_stash(s_glob, ot, psum_ap, mode, wts):
        slot = (s_glob + ot) % 4
        stg = ebB[:, slot * 512:(slot + 1) * 512]
        w2 = flat(wts, stage_free[slot])
        if mode == "plain":
            vv = emit("vector", lambda e, p=psum_ap, o=stg: e.tensor_copy(out=o, in_=p),
                      inc="vS", waits=w2)
        elif mode == "bias":
            vv = emit("vector", lambda e, p=psum_ap, o=stg, t=ot: e.tensor_tensor(
                out=o, in0=p, in1=ebA[:, t * 512:(t + 1) * 512], op=ALU.add), inc="vS", waits=w2)
        else:
            vv = emit("vector", lambda e, p=psum_ap, o=stg, t=ot: e.tensor_tensor(
                out=o, in0=p, in1=ebA[:, t * 512:(t + 1) * 512], op=ALU.mult), inc="vS", waits=w2)
        cw = dma("gpsimd", f"sg{slot}", stash_d[s_glob * 128:(s_glob + 1) * 128, ot * 512:(ot + 1) * 512],
                 stg, waits=[("vS", vv)])
        stage_free[slot] = [(f"sg{slot}", cw)]

    def run_outproj(NK, mode, movb_d, src_d, src_ready, bias_d, wready):
        fence_wbf()
        if mode in ("bias", "sdown"):
            cA = dma("sync", "eA", ebA, bias_d[:, :], waits=flat(ev["ebA_free"]))
            ev["ebA_bias"] = [("eA", cA)]
        for ot in range(4):
            for sbi in range(4):
                sb = [sbi * 4 + j for j in range(4)]
                for k in range(NK):
                    jw, wrdy = ring_load(k, movb_d[k, :, ot * 512:(ot + 1) * 512], extra=wready)
                    js, srdy = stat_ring_load(k, src_d[k, :, sb[0] * 128: sb[0] * 128 + 512],
                                              extra=src_ready)
                    for j in range(4):
                        w_ = flat(wrdy, srdy, scor_free[0] if (k == 0 and j == 0) else [])
                        emit("tensor", lambda e, j_=j, jw_=jw, js_=js, k_=k, NK_=NK: e.matmul(
                            out=scor[:, j_ * 512:(j_ + 1) * 512],
                            lhsT=arena[:, OP_CT + js_ * 512 + j_ * 128: OP_CT + js_ * 512 + j_ * 128 + 128],
                            rhs=wbf[:, jw_ * 512:(jw_ + 1) * 512],
                            start=(k_ == 0), stop=(k_ == NK_ - 1)),
                            inc="tS" if j == 3 else None,
                            waits=w_ if j == 0 else ())
                    ring_free[jw] = [("tS", ctr["tS"])]
                    so_free[js] = [("tS", ctr["tS"])]
                tl = ctr["tS"]
                for j, s_g in enumerate(sb):
                    wts = [("tS", tl)]
                    if mode in ("bias", "sdown"):
                        wts = flat(wts, ev["ebA_bias"])
                    evac_stash(s_g, ot, scor[:, j * 512:(j + 1) * 512], mode, wts)
                scor_free[0] = [("vS", ctr["vS"])]
        ev["ebA_free"] = [("vS", ctr["vS"])]
        ev["ebB_free"] = [(f"sg{j}", ctr[f"sg{j}"]) for j in range(4)]

    def epilogue(sub):
        src = x_d if sub == 0 else xres_d
        dst = out_d if sub == 3 else xres_d
        wsem = "xw0" if sub % 2 == 0 else "xw1"
        rsem = "xw1" if sub % 2 == 0 else "xw0"
        stash_done = [(f"sg{j}", ctr[f"sg{j}"]) for j in range(4)]
        src_done = [] if sub == 0 else [(rsem, ctr[rsem])]
        last = None
        for s in range(NS):
            cA = dma("sync", "eA", ebA, stash_d[s * 128:(s + 1) * 128, :],
                     waits=flat(ev["ebA_free"], stash_done))
            cB = dma("sync", "eB", ebB, src[s * 128:(s + 1) * 128, :],
                     waits=flat(ev["ebB_free"], src_done))
            p1 = emit("gpsimd", lambda e: e.tensor_tensor(out=ebA, in0=ebA, in1=ebB, op=ALU.add),
                      inc="pS", waits=[("eA", cA), ("eB", cB)])
            ev["ebB_free"] = [("pS", p1)]
            cw = dma("gpsimd", wsem, dst[s * 128:(s + 1) * 128, :], ebA, waits=[("pS", p1)])
            last = cw
            if sub < 3:
                ln_chain(s, [("pS", p1)], ebA)
                ev["ebA_free"] = [("pS", ctr["pS"]), (wsem, cw)]
            else:
                ev["ebA_free"] = [(wsem, cw)]
        if sub < 3:
            ev["xnf"] = [("vS", ctr["vS"])]
        return last

    def ffn():
        fence_wbf()
        for sg in range(4):
            for icb in range(16):
                for kc in range(NCH):
                    jw, wrdy = ring_load(kc, wgb_d[kc, :, icb * 512:(icb + 1) * 512],
                                         extra=pcw["ffn"])
                    for j in range(4):
                        w_ = flat(wrdy, scor_free[0] if (kc == 0 and j == 0) else [],
                                  ev["xnf"] if (kc == 0 and j == 0) else [])
                        emit("tensor", lambda e, j_=j, jw_=jw, kc_=kc, sg_=sg: e.matmul(
                            out=scor[:, j_ * 512:(j_ + 1) * 512],
                            lhsT=wbf[:, jw_ * 512 + j_ * 128: jw_ * 512 + j_ * 128 + 128],
                            rhs=xnf[:, kc_ * S + sg_ * 512: kc_ * S + (sg_ + 1) * 512],
                            start=(kc_ == 0), stop=(kc_ == NCH - 1)),
                            inc="tS" if j == 3 else None,
                            waits=w_ if j == 0 else ())
                    ring_free[jw] = [("tS", ctr["tS"])]
                gate_stop = ctr["tS"]
                silu_ev = []
                for j in range(4):
                    ic = icb * 4 + j
                    aa = emit("scalar", lambda e, j_=j, ic_=ic: e.activation(
                        out=ebB[:, j_ * 512:(j_ + 1) * 512], in_=scor[:, j_ * 512:(j_ + 1) * 512],
                        func=AF.Silu, scale=sgc[:, ic_:ic_ + 1]),
                        inc="aS", waits=flat([("tS", gate_stop)], ev["ebB_free"] if j == 0 else []))
                    silu_ev.append(aa)
                scor_free[0] = [("aS", silu_ev[-1])]
                for kc in range(NCH):
                    jw, wrdy = ring_load(kc, wub_d[kc, :, icb * 512:(icb + 1) * 512])
                    for j in range(4):
                        w_ = flat(wrdy, scor_free[0] if (kc == 0 and j == 0) else [])
                        emit("tensor", lambda e, j_=j, jw_=jw, kc_=kc, sg_=sg: e.matmul(
                            out=scor[:, j_ * 512:(j_ + 1) * 512],
                            lhsT=wbf[:, jw_ * 512 + j_ * 128: jw_ * 512 + j_ * 128 + 128],
                            rhs=xnf[:, kc_ * S + sg_ * 512: kc_ * S + (sg_ + 1) * 512],
                            start=(kc_ == 0), stop=(kc_ == NCH - 1)),
                            inc="tS" if j == 3 else None,
                            waits=w_ if j == 0 else ())
                    ring_free[jw] = [("tS", ctr["tS"])]
                up_stop = ctr["tS"]
                hv = []
                for j in range(4):
                    ic = icb * 4 + j
                    vv = emit("vector", lambda e, j_=j, ic_=ic: e.tensor_scalar(
                        out=ebA[:, j_ * 512:(j_ + 1) * 512], in0=scor[:, j_ * 512:(j_ + 1) * 512],
                        scalar1=suc[:, ic_:ic_ + 1], scalar2=None, op0=ALU.mult),
                        inc="vS", waits=flat([("tS", up_stop)], ev["ebA_free"] if j == 0 else []))
                    pp = emit("gpsimd", lambda e, j_=j, ic_=ic: e.tensor_tensor(
                        out=arena[:, F_HT + ic_ * 512: F_HT + (ic_ + 1) * 512],
                        in0=ebB[:, j_ * 512:(j_ + 1) * 512], in1=ebA[:, j_ * 512:(j_ + 1) * 512], op=ALU.mult),
                        inc="pS", waits=[("aS", silu_ev[j]), ("vS", vv)])
                    hv.append(pp)
                scor_free[0] = [("vS", ctr["vS"])]
                ev["ebB_free"] = [("pS", hv[-1])]
                ev["ebA_free"] = [("pS", hv[-1])]
            ev["hT"] = [("pS", ctr["pS"])]
            cA = dma("sync", "eA", ebA, sdbc_d[:, :], waits=flat(ev["ebA_free"]))
            ev["ebA_bias"] = [("eA", cA)]
            sb = [sg * 4 + j for j in range(4)]
            for ot in range(4):
                for k in range(NIC):
                    jw, wrdy = ring_load(k, wdb_d[k, :, ot * 512:(ot + 1) * 512])
                    for j in range(4):
                        w_ = flat(wrdy, ev["hT"],
                                  scor_free[0] if (k == 0 and j == 0) else [])
                        emit("tensor", lambda e, j_=j, jw_=jw, k_=k: e.matmul(
                            out=scor[:, j_ * 512:(j_ + 1) * 512],
                            lhsT=arena[:, F_HT + k_ * 512 + j_ * 128: F_HT + k_ * 512 + j_ * 128 + 128],
                            rhs=wbf[:, jw_ * 512:(jw_ + 1) * 512],
                            start=(k_ == 0), stop=(k_ == NIC - 1)),
                            inc="tS" if j == 3 else None,
                            waits=w_ if j == 0 else ())
                    ring_free[jw] = [("tS", ctr["tS"])]
                tl = ctr["tS"]
                for j, s_g in enumerate(sb):
                    evac_stash(s_g, ot, scor[:, j * 512:(j + 1) * 512], "sdown",
                               flat([("tS", tl)], ev["ebA_bias"]))
                scor_free[0] = [("vS", ctr["vS"])]
            ev["ebA_free"] = [("vS", ctr["vS"])]
            ev["ebB_free"] = [(f"sg{j}", ctr[f"sg{j}"]) for j in range(4)]

    def attn_b():
        fence_wbf()
        ev["m1_free"] = flat(ev["ebA_free"])
        ev["m2_free"] = flat(ev["ebA_free"])
        for kk in range(4):
            qkv_v = []
            for kc in range(NCH):
                load_blk(kc, wattnb_d[kc, :, kk * 512:(kk + 1) * 512],
                         extra=pcw["wattn"] if kc == 0 else ())
            wrdy = flat(*[wbf_ready[x] for x in range(NCH)])
            for hh in range(4):
                for stt in range(NT):
                    pp = next_psum()
                    tl = proj_accum(pp,
                                    [wbf[:, kc * 512 + hh * 128: kc * 512 + hh * 128 + 128] for kc in range(NCH)],
                                    [xnf[:, kc * S + stt * 512: kc * S + (stt + 1) * 512] for kc in range(NCH)],
                                    extra_waits=flat(wrdy, ev["xnf"]))
                    fe, re = rope_evac(pp, stt, arena[:, B_Q4 + hh * S + stt * 512: B_Q4 + hh * S + (stt + 1) * 512], tl)
                    psum_free[id(pp)] = fe
                    qkv_v.append(re[0][1])
            for kc in range(NCH):
                wbf_free[kc] = [("tS", ctr["tS"])]
            for kc in range(NCH):
                load_blk(kc, wattnb_d[kc, :, 2048 + kk * 128: 2048 + (kk + 1) * 128], w=128)
            wrdy = flat(*[wbf_ready[x] for x in range(NCH)])
            for stt in range(NT):
                pp = next_psum()
                tl = proj_accum(pp,
                                [wbf[:, kc * 512: kc * 512 + 128] for kc in range(NCH)],
                                [xnf[:, kc * S + stt * 512: kc * S + (stt + 1) * 512] for kc in range(NCH)],
                                extra_waits=wrdy)
                fe, re = rope_evac(pp, stt, arena[:, B_KB + stt * 512: B_KB + (stt + 1) * 512], tl)
                psum_free[id(pp)] = fe
                qkv_v.append(re[0][1])
            for kc in range(NCH):
                wbf_free[kc] = [("tS", ctr["tS"])]
            for kc in range(NCH):
                load_blk(kc, wattnb_d[kc, :, 2560 + kk * 512: 2560 + (kk + 1) * 512])
            wrdy = flat(*[wbf_ready[x] for x in range(NCH)])
            for ssub in range(NS):
                pp = next_psum()
                tl = proj_accum(pp,
                                [xnf[:, kc * S + ssub * 128: kc * S + ssub * 128 + 128] for kc in range(NCH)],
                                [wbf[:, kc * 512:(kc + 1) * 512] for kc in range(NCH)],
                                extra_waits=wrdy)
                vv = emit("vector", lambda e, pp_=pp, s_=ssub: e.tensor_copy(
                    out=arena[:, B_V4 + s_ * 512: B_V4 + (s_ + 1) * 512], in_=pp_[:, 0:512]),
                    inc="vS", waits=[("tS", tl)])
                psum_free[id(pp)] = [("vS", vv)]
                qkv_v.append(vv)
            for kc in range(NCH):
                wbf_free[kc] = [("tS", ctr["tS"])]
            ev["qkv"] = [("vS", max(qkv_v))]
            for hh in range(4):
                h = 4 * kk + hh
                for J in range(4):
                    for p in range(4):
                        softmax_transpose(4, J, p, False, B_Q4 + hh * S, B_KB, B_PT, B_PR)
                    for dvc in range(4):
                        pp = next_psum()
                        tl = proj_accum(pp,
                                        [arena[:, B_V4 + kc * 512 + dvc * 128: B_V4 + kc * 512 + dvc * 128 + 128] for kc in range(16)],
                                        [arena[:, B_PT + kc * 512: B_PT + (kc + 1) * 512] for kc in range(16)],
                                        extra_waits=flat(probs_free[0], ev["ctxs_free"]))
                        vv = emit("vector", lambda e, pp_=pp, J_=J, d_=dvc: e.tensor_copy(
                            out=arena[:, B_CS + d_ * 2048 + J_ * 512: B_CS + d_ * 2048 + (J_ + 1) * 512],
                            in_=pp_[:, 0:512]),
                            inc="vS", waits=[("tS", tl)])
                        psum_free[id(pp)] = [("vS", vv)]
                cw = None
                for dvc in range(4):
                    cw = dma("gpsimd", "cw", ctxb_d[h * 4 + dvc, :, :],
                             arena[:, B_CS + dvc * 2048: B_CS + (dvc + 1) * 2048],
                             waits=[("vS", ctr["vS"])])
                ev["ctxs_free"] = [("cw", cw)]
        ev["ebA_free"] = flat(ev["m1_free"], ev["m2_free"])

    attn_a()
    ctxa_done = [("cw", ctr["cw"])]
    run_outproj(16, "plain", wotb_d, ctxa_d, ctxa_done, None, pcw["wot"])
    epilogue(0)
    ffn()
    epilogue(1)
    attn_b()
    ctxb_done = [("cw", ctr["cw"])]
    run_outproj(64, "bias", woutb_d, ctxb_d, ctxb_done, bbc_d, pcw["wout"])
    epilogue(2)
    ffn()
    epilogue(3)

    emit("gpsimd", lambda e: e.memset(cols[:, 15:16], 0.0),
         waits=[(k, ctr[k]) for k in SEM_NAMES if ctr[k] > 0])

    with nc.Block() as block:
        def mk(ename):
            def sec(eng):
                for ws, fn, inc, n in streams[ename]:
                    for (s_, v) in ws:
                        eng.wait_ge(sems[s_], v)
                    inst = fn(eng)
                    if inc is not None:
                        inst.then_inc(sems[inc], n)
            sec.__name__ = ename
            return sec
        block.tensor(mk("tensor"))
        block.vector(mk("vector"))
        block.scalar(mk("scalar"))
        block.gpsimd(mk("gpsimd"))
        block.sync(mk("sync"))
    est.close()
    n_inst = sum(len(v) for v in streams.values())
    return nc, n_inst


def _host_constants():
    HD = 128
    TH = 10000.0
    inv = 1.0 / TH ** (np.arange(0, HD, 2) / HD)
    t = np.arange(S)
    ang_half = np.outer(inv, t)
    ang = np.concatenate([ang_half, ang_half], 0)
    cosb = np.cos(ang).astype(ml_dtypes.bfloat16)
    sin = np.sin(ang)
    sinS = np.concatenate([-sin[:64], sin[64:]], 0).astype(ml_dtypes.bfloat16)
    qq = np.arange(128)[:, None]
    m = np.zeros((128, 2048), dtype=np.float32)
    for p in range(4):
        kkk = np.arange(512)[None, :]
        m[:, p * 512:(p + 1) * 512] = (kkk <= p * 128 + qq).astype(np.float32)
    mask01 = m.astype(ml_dtypes.bfloat16)
    ident = np.eye(128, dtype=np.float32).astype(ml_dtypes.bfloat16)
    return cosb, sinS, mask01, ident


_CACHE = {}


def kernel(L0_attn_a_hidden_states, L0_attn_a_Wq, L0_attn_a_Wk, L0_attn_a_Wv, L0_attn_a_Wo,
           L0_attn_b_W_attn, L0_attn_b_W_out, L0_attn_b_b_out,
           L0_ffn_w_gate, L0_ffn_w_up, L0_ffn_w_down,
           L0_ffn_s_gate, L0_ffn_s_up, L0_ffn_s_down):
    x = np.asarray(L0_attn_a_hidden_states, dtype=np.float32)
    if "nc" not in _CACHE:
        _CACHE["nc"], _CACHE["n"] = build_program()
    nc = _CACHE["nc"]
    cosb, sinS, mask01, ident = _host_constants()
    wqt = np.ascontiguousarray(np.asarray(L0_attn_a_Wq, np.float32).T)
    wkt = np.ascontiguousarray(np.asarray(L0_attn_a_Wk, np.float32).T)
    wvt = np.ascontiguousarray(np.asarray(L0_attn_a_Wv, np.float32).T)
    wot = np.ascontiguousarray(np.asarray(L0_attn_a_Wo, np.float32).T)
    wattn = np.ascontiguousarray(np.asarray(L0_attn_b_W_attn, np.float32))
    wout = np.ascontiguousarray(np.asarray(L0_attn_b_W_out, np.float32))
    wg8 = np.ascontiguousarray(np.asarray(L0_ffn_w_gate, np.int8))
    wu8 = np.ascontiguousarray(np.asarray(L0_ffn_w_up, np.int8))
    wd8 = np.ascontiguousarray(np.asarray(L0_ffn_w_down, np.int8))
    sgc = np.ascontiguousarray(np.asarray(L0_ffn_s_gate, np.float32).reshape(64, 128).T)
    suc = np.ascontiguousarray(np.asarray(L0_ffn_s_up, np.float32).reshape(64, 128).T)
    sdbc = np.ascontiguousarray(np.broadcast_to(np.asarray(L0_ffn_s_down, np.float32)[None, :], (128, D)))
    bbc = np.ascontiguousarray(np.broadcast_to(np.asarray(L0_attn_b_b_out, np.float32)[None, :], (128, D)))
    shared = dict(wqt=wqt, wkt=wkt, wvt=wvt, wot=wot, wattn=wattn, wout=wout,
                  wg8=wg8, wu8=wu8, wd8=wd8, sgc=sgc, suc=suc, sdbc=sdbc, bbc=bbc,
                  cosb=cosb, sinb=sinS, mask01=mask01, ident=ident)
    in_maps = []
    for c in range(8):
        m = dict(shared)
        m["x"] = np.ascontiguousarray(x[c])
        in_maps.append(m)
    res = run_bass_kernel_spmd(nc, in_maps, core_ids=list(range(8)))
    return np.stack([np.asarray(r["out"]) for r in res.results], axis=0)


if __name__ == "__main__":
    nc, n = build_program()
    print("built", n, "instructions")


# version 04 (final)
# speedup vs baseline: 1.8469x; 1.0225x over previous
import math
import numpy as np
import ml_dtypes
from contextlib import ExitStack
import concourse.bass as bass
import concourse.mybir as mybir
from concourse.bass_utils import run_bass_kernel_spmd

F32 = mybir.dt.float32
BF16 = mybir.dt.bfloat16
I8 = mybir.dt.int8
AF = mybir.ActivationFunctionType
ALU = mybir.AluOpType
AX = mybir.AxisListType

S = 2048
D = 2048
NCH = 16
NS = 16
NT = 4
INTER = 8192
NIC = 64
SCALE = 1.0 / math.sqrt(128.0)
EPS = 1e-5

ARENA_COLS = 36864
A_Q4, A_K4, A_V4, A_PT, A_PR, A_CS = 0, 8192, 16384, 24576, 32768, 34816
B_Q4, B_KB, B_V4, B_PT, B_PR, B_CS = 0, 8192, 10240, 18432, 26624, 28672
F_HT = 0
XN_TM = 32768
OP_CT = 0


def build_program():
    nc = bass.Bass("TRN2", target_bir_lowering=False)

    def din(name, shape, dt):
        return nc.dram_tensor(name, shape, dt, kind="ExternalInput").ap()

    x_d = din("x", [S, D], F32)
    wqt_d = din("wqt", [D, D], F32)
    wkt_d = din("wkt", [D, D], F32)
    wvt_d = din("wvt", [D, D], F32)
    wot_d = din("wot", [D, D], F32)
    wattn_d = din("wattn", [D, 4608], F32)
    wout_d = din("wout", [INTER, D], F32)
    wg8_d = din("wg8", [D, INTER], I8)
    wu8_d = din("wu8", [D, INTER], I8)
    wd8_d = din("wd8", [INTER, D], I8)
    sgc_d = din("sgc", [128, 64], F32)
    suc_d = din("suc", [128, 64], F32)
    sdbc_d = din("sdbc", [128, D], F32)
    bbc_d = din("bbc", [128, D], F32)
    cosb_d = din("cosb", [128, S], BF16)
    sinb_d = din("sinb", [128, S], BF16)
    mask_d = din("mask01", [128, D], BF16)
    id_d = din("ident", [128, 128], BF16)

    out_d = nc.dram_tensor("out", [S, D], F32, kind="ExternalOutput").ap()
    xres_d = nc.dram_tensor("xres", [S, D], F32, kind="Internal").ap()
    stash_d = nc.dram_tensor("stash", [S, D], F32, kind="Internal").ap()
    wqtb_d = nc.dram_tensor("wqtb", [4, 128, 16 * 512], BF16, kind="Internal").ap()
    wktb_d = nc.dram_tensor("wktb", [4, 128, 16 * 512], BF16, kind="Internal").ap()
    wvtb_d = nc.dram_tensor("wvtb", [4, 128, 16 * 512], BF16, kind="Internal").ap()
    wotb_d = nc.dram_tensor("wotb", [4, 128, 16 * 512], BF16, kind="Internal").ap()
    wabq_d = nc.dram_tensor("wabq", [4, 128, 16 * 512], BF16, kind="Internal").ap()
    wabk_d = nc.dram_tensor("wabk", [4, 128, 16 * 128], BF16, kind="Internal").ap()
    wabv_d = nc.dram_tensor("wabv", [4, 128, 16 * 512], BF16, kind="Internal").ap()
    woutb_d = nc.dram_tensor("woutb", [4, 128, 64 * 512], BF16, kind="Internal").ap()
    wgb_d = nc.dram_tensor("wgb", [16, 128, 16 * 512], BF16, kind="Internal").ap()
    wub_d = nc.dram_tensor("wub", [16, 128, 16 * 512], BF16, kind="Internal").ap()
    wdb_d = nc.dram_tensor("wdb", [4, 128, 64 * 512], BF16, kind="Internal").ap()
    ctxa2_d = nc.dram_tensor("ctxa2", [4, 128, 16 * 512], BF16, kind="Internal").ap()
    ctxb2_d = nc.dram_tensor("ctxb2", [4, 128, 64 * 512], BF16, kind="Internal").ap()

    est = ExitStack()
    sems = {}
    SEM_NAMES = (["cN", "eA", "eB", "wb", "rp", "cw", "xw0", "xw1"]
                 + [f"pf{j}" for j in range(2)]
                 + [f"sg{j}" for j in range(4)]
                 + [f"wi{j}" for j in range(8)]
                 + [f"r{j}" for j in range(8)]
                 + [f"so{j}" for j in range(8)]
                 + [f"pc{j}" for j in range(4)]
                 + ["tS", "vS", "aS", "pS"])
    for k in SEM_NAMES:
        sems[k] = est.enter_context(nc.semaphore(k))
    DMA_SEMS = set(SEM_NAMES) - {"tS", "vS", "aS", "pS"}

    xnf = nc.alloc_sbuf_tensor("xnf", [128, NCH * S], BF16).ap()
    cosb = nc.alloc_sbuf_tensor("sb_cosb", [128, S], BF16).ap()
    sinb = nc.alloc_sbuf_tensor("sb_sinb", [128, S], BF16).ap()
    ident = nc.alloc_sbuf_tensor("sb_ident", [128, 128], BF16).ap()
    mask01 = nc.alloc_sbuf_tensor("sb_mask01", [128, D], BF16).ap()
    sgc = nc.alloc_sbuf_tensor("sb_sgc", [128, 64], F32).ap()
    suc = nc.alloc_sbuf_tensor("sb_suc", [128, 64], F32).ap()
    cols = nc.alloc_sbuf_tensor("cols", [128, 16], F32).ap()
    arena = nc.alloc_sbuf_tensor("arena", [128, ARENA_COLS], BF16).ap()
    ebA = nc.alloc_sbuf_tensor("ebA", [128, D], F32).ap()
    ebB = nc.alloc_sbuf_tensor("ebB", [128, D], F32).ap()
    ropeb = nc.alloc_sbuf_tensor("ropeb", [128, 1024], BF16).ap()
    wbf = nc.alloc_sbuf_tensor("wbf", [128, 16 * 512], BF16).ap()
    wsti = nc.alloc_sbuf_tensor("wsti", [128, 8 * 512], I8).ap()
    pcv = nc.alloc_sbuf_tensor("pcv", [128, 4 * 512], BF16).ap()
    pcf = nc.alloc_sbuf_tensor("pcf", [128, 2 * 512], F32).ap()

    scor = nc.alloc_psum_tensor("scor", [128, 2048], F32).ap()
    pA = nc.alloc_psum_tensor("pA", [128, 512], F32).ap()
    pB = nc.alloc_psum_tensor("pB", [128, 512], F32).ap()
    pC = nc.alloc_psum_tensor("pC", [128, 512], F32).ap()
    ptp = nc.alloc_psum_tensor("ptp", [128, 512], BF16).ap()

    streams = {e: [] for e in ("tensor", "vector", "scalar", "gpsimd", "sync")}
    ctr = {k: 0 for k in sems}
    water = {e: {k: 0 for k in sems} for e in streams}

    def flat(*evs):
        out = []
        for e in evs:
            if e is None:
                continue
            if isinstance(e, list):
                out.extend(x for x in e if x is not None)
            else:
                out.append(e)
        return out

    def emit(eng, fn, inc=None, waits=()):
        byname = {}
        for (s_, v) in waits:
            if v > byname.get(s_, 0):
                byname[s_] = v
        ws = []
        for s_, v in byname.items():
            if v > water[eng][s_]:
                ws.append((s_, v))
                water[eng][s_] = v
        n = 0
        if inc is not None:
            n = 16 if inc in DMA_SEMS else 1
            ctr[inc] += n
        streams[eng].append((ws, fn, inc, n))
        return ctr[inc] if inc else None

    def dma(q, sem, out, in_, waits=()):
        return emit(q, lambda e, o=out, i=in_: e.dma_start(out=o, in_=i), inc=sem, waits=waits)

    ev = {
        "xnf": [], "ebA_free": [], "ebB_free": [], "ptp_free": [], "ln_free": [],
        "ropeb_free": [], "m1_free": [], "m2_free": [], "qkv": [], "ctxs_free": [],
        "consts": [], "hT": [], "ebA_bias": [],
    }

    pcw = {}
    _pcf_free = [[], []]
    _pcf_next = [0]
    _wsti_free = [[] for _ in range(8)]
    _wsti_next = [0]
    _pcv_free = [[] for _ in range(4)]
    _pcv_next = [0]

    def _pcv_slot():
        j = _pcv_next[0]; _pcv_next[0] = (j + 1) % 4
        return j

    def preconv_f32(src_tile, dst_tile, w=512):
        j = _pcf_next[0]; _pcf_next[0] = (j + 1) % 2
        c = dma("scalar", f"pf{j}", pcf[:, j * 512:j * 512 + w], src_tile, waits=_pcf_free[j])
        v = _pcv_slot()
        aa = emit("scalar", lambda e, jj=j, vv=v, w_=w: e.copy(out=pcv[:, vv * 512:vv * 512 + w_],
                                                               in_=pcf[:, jj * 512:jj * 512 + w_]),
                  inc="aS", waits=flat([(f"pf{j}", c)], _pcv_free[v]))
        _pcf_free[j] = [("aS", aa)]
        cw = dma("gpsimd", f"pc{v}", dst_tile, pcv[:, v * 512:v * 512 + w], waits=[("aS", aa)])
        _pcv_free[v] = [(f"pc{v}", cw)]

    def preconv_i8(src_tile, dst_tile):
        j = _wsti_next[0]; _wsti_next[0] = (j + 1) % 8
        c = dma("scalar", f"wi{j}", wsti[:, j * 512:(j + 1) * 512], src_tile, waits=_wsti_free[j])
        v = _pcv_slot()
        pp = emit("gpsimd", lambda e, jj=j, vv=v: e.tensor_copy(out=pcv[:, vv * 512:(vv + 1) * 512],
                                                                in_=wsti[:, jj * 512:(jj + 1) * 512]),
                  inc="pS", waits=flat([(f"wi{j}", c)], _pcv_free[v]))
        _wsti_free[j] = [("pS", pp)]
        cw = dma("gpsimd", f"pc{v}", dst_tile, pcv[:, v * 512:(v + 1) * 512], waits=[("pS", pp)])
        _pcv_free[v] = [(f"pc{v}", cw)]

    def pcw_mark(name):
        pcw[name] = [(f"pc{j}", ctr[f"pc{j}"]) for j in range(4)]

    def emit_preconvert_qkv_weights():
        for g in range(4):
            for (srcd, dstd) in ((wqt_d, wqtb_d), (wkt_d, wktb_d), (wvt_d, wvtb_d)):
                for kc in range(NCH):
                    preconv_f32(srcd[kc * 128:(kc + 1) * 128, g * 512:(g + 1) * 512],
                                dstd[g, :, kc * 512:(kc + 1) * 512])
            pcw_mark(f"qkv{g}")

    def emit_preconvert_rest():
        for kc in range(NCH):
            for j in range(4):
                preconv_f32(wot_d[kc * 128:(kc + 1) * 128, j * 512:(j + 1) * 512],
                            wotb_d[j, :, kc * 512:(kc + 1) * 512])
        pcw_mark("wot")
        for kc in range(NCH):
            for j in range(16):
                preconv_i8(wg8_d[kc * 128:(kc + 1) * 128, j * 512:(j + 1) * 512],
                           wgb_d[j, :, kc * 512:(kc + 1) * 512])
                preconv_i8(wu8_d[kc * 128:(kc + 1) * 128, j * 512:(j + 1) * 512],
                           wub_d[j, :, kc * 512:(kc + 1) * 512])
        for k in range(NIC):
            for j in range(4):
                preconv_i8(wd8_d[k * 128:(k + 1) * 128, j * 512:(j + 1) * 512],
                           wdb_d[j, :, k * 512:(k + 1) * 512])
        pcw_mark("ffn")
        for kk in range(4):
            for kc in range(NCH):
                preconv_f32(wattn_d[kc * 128:(kc + 1) * 128, kk * 512:(kk + 1) * 512],
                            wabq_d[kk, :, kc * 512:(kc + 1) * 512])
            for kc in range(NCH):
                preconv_f32(wattn_d[kc * 128:(kc + 1) * 128, 2048 + kk * 128: 2048 + (kk + 1) * 128],
                            wabk_d[kk, :, kc * 128:(kc + 1) * 128], w=128)
            for kc in range(NCH):
                preconv_f32(wattn_d[kc * 128:(kc + 1) * 128, 2560 + kk * 512: 2560 + (kk + 1) * 512],
                            wabv_d[kk, :, kc * 512:(kc + 1) * 512])
        pcw_mark("wattn")
        for k in range(NIC):
            for j in range(4):
                preconv_f32(wout_d[k * 128:(k + 1) * 128, j * 512:(j + 1) * 512],
                            woutb_d[j, :, k * 512:(k + 1) * 512])
        pcw_mark("wout")

    wbf_free = [[] for _ in range(16)]
    wbf_ready = [[] for _ in range(16)]

    def load_blk_all(dram_slice, ncols, extra=()):
        c = dma("sync", "wb", wbf[:, 0:ncols], dram_slice,
                waits=flat(*[wbf_free[x] for x in range(16)], list(extra)))
        for x in range(16):
            wbf_ready[x] = [("wb", c)]
        return [("wb", c)]

    ring_free = [[] for _ in range(2)]

    def ring_half_load(half, dram_slice, extra=()):
        c = dma("sync", f"r{half}", wbf[:, half * 4096:(half + 1) * 4096], dram_slice,
                waits=flat(ring_free[half], list(extra)))
        return [(f"r{half}", c)]

    so_free = [[], []]

    def fence_wbf():
        e_ = [("tS", ctr["tS"])]
        for j in range(16):
            wbf_free[j] = list(e_)
        for j in range(2):
            ring_free[j] = list(e_)
            so_free[j] = list(e_)

    def stat_half_load(half, dram_slice, extra=()):
        c = dma("sync", f"so{half}", arena[:, OP_CT + half * 4096: OP_CT + (half + 1) * 4096],
                dram_slice, waits=flat(so_free[half], list(extra)))
        return [(f"so{half}", c)]

    dma("sync", "cN", cosb, cosb_d[:, :])
    dma("sync", "cN", sinb, sinb_d[:, :])
    dma("sync", "cN", ident, id_d[:, :])
    dma("sync", "cN", mask01, mask_d[:, :])
    dma("sync", "cN", sgc, sgc_d[:, :])
    c = dma("sync", "cN", suc, suc_d[:, :])
    ev["consts"] = [("cN", c)]

    def ln_chain(s, pre_waits, buf):
        pw = flat(pre_waits, ev["ln_free"])
        v1 = emit("vector", lambda e, b=buf: e.reduce_sum(out=cols[:, 0:1], in_=b, axis=AX.X),
                  inc="vS", waits=pw)
        a1 = emit("scalar", lambda e, b=buf: e.activation(
            out=arena[:, XN_TM:XN_TM + 2048], in_=b, func=AF.Square, accum_out=cols[:, 1:2]),
            inc="aS", waits=pw)
        a2 = emit("scalar", lambda e: e.mul(out=cols[:, 2:3], in_=cols[:, 0:1], mul=1.0 / D),
                  inc="aS", waits=[("vS", v1)])
        v2 = emit("vector", lambda e: e.tensor_scalar(out=cols[:, 3:4], in0=cols[:, 2:3],
                                                      scalar1=cols[:, 2:3], scalar2=-EPS,
                                                      op0=ALU.mult, op1=ALU.add),
                  inc="vS", waits=[("aS", a2)])
        p1 = emit("gpsimd", lambda e: e.tensor_scalar(out=cols[:, 4:5], in0=cols[:, 1:2],
                                                      scalar1=1.0 / D, scalar2=cols[:, 3:4],
                                                      op0=ALU.mult, op1=ALU.subtract),
                  inc="pS", waits=[("aS", a1), ("vS", v2)])
        a3 = emit("scalar", lambda e: e.activation(out=cols[:, 5:6], in_=cols[:, 4:5],
                                                   func=AF.Sqrt),
                  inc="aS", waits=[("pS", p1)])
        v3 = emit("vector", lambda e: e.reciprocal(out=cols[:, 6:7], in_=cols[:, 5:6]),
                  inc="vS", waits=[("aS", a3)])
        p2 = emit("gpsimd", lambda e, b=buf: e.tensor_scalar(
            out=arena[:, XN_TM:XN_TM + 2048], in0=b,
            scalar1=cols[:, 2:3], scalar2=cols[:, 6:7], op0=ALU.subtract, op1=ALU.mult),
            inc="pS", waits=[("vS", v3), ("aS", a1)])
        last_t = 0
        xnfv = xnf.rearrange("a (b c) -> a b c", b=NCH)
        last_v = None
        for g4 in range(4):
            t = None
            for j in range(4):
                kc = g4 * 4 + j
                t = emit("tensor", lambda e, kk=kc, jj=j: e.transpose(
                    out=ptp[:, jj * 128:(jj + 1) * 128],
                    in_=arena[:, XN_TM + kk * 128:XN_TM + (kk + 1) * 128], identity=ident),
                    inc="tS" if j == 3 else None,
                    waits=flat([("pS", p2)], ev["ptp_free"], ev["consts"]) if j == 0 else ())
            last_t = t
            src = ptp.rearrange("a (b c) -> a b c", b=4)
            dst = xnfv[:, g4 * 4:g4 * 4 + 4, s * 128:(s + 1) * 128]
            last_v = emit("vector", lambda e, sr=src, dt_=dst: e.tensor_copy(out=dt_, in_=sr),
                          inc="vS", waits=[("tS", t)])
            ev["ptp_free"] = [("vS", last_v)]
        ev["ln_free"] = [("pS", p2), ("tS", last_t), ("vS", last_v)]
        return last_v

    emit_preconvert_qkv_weights()
    xnf_last = 0
    for s in range(NS):
        buf, key, sm = (ebA, "ebA_free", "eA") if s % 2 == 0 else (ebB, "ebB_free", "eB")
        c = dma("sync", sm, buf, x_d[s * 128:(s + 1) * 128, :], waits=flat(ev[key], ev["consts"]))
        xnf_last = ln_chain(s, [(sm, c)], buf)
        ev[key] = [("pS", ctr["pS"])]
    ev["xnf"] = [("vS", xnf_last)]
    emit_preconvert_rest()

    psum_rot = [pA, pB, pC]
    psum_free = {id(pA): [], id(pB): [], id(pC): []}
    rot_i = [0]

    def next_psum():
        p = psum_rot[rot_i[0] % 3]
        rot_i[0] += 1
        return p

    def proj_accum(dst_psum, stat_slices, rhs_slices, extra_waits=()):
        n = len(stat_slices)
        tlast = None
        for k in range(n):
            tlast = emit("tensor", lambda e, a=stat_slices[k], b=rhs_slices[k], k_=k, n_=n,
                         dp=dst_psum: e.matmul(out=dp[:, 0:512], lhsT=a, rhs=b,
                                               start=(k_ == 0), stop=(k_ == n_ - 1)),
                         inc="tS" if k == n - 1 else None,
                         waits=flat(list(extra_waits), psum_free[id(dst_psum)]) if k == 0 else ())
        return tlast

    def rope_evac(psum, stt, dst_bf, mm_done):
        a1 = emit("scalar", lambda e, p=psum: e.copy(out=ropeb[:, 0:512], in_=p[:, 0:512]),
                  inc="aS", waits=flat([("tS", mm_done)], ev["ropeb_free"]))
        c1 = dma("gpsimd", "rp", ropeb[0:64, 512:1024], ropeb[64:128, 0:512], waits=[("aS", a1)])
        c2 = dma("gpsimd", "rp", ropeb[64:128, 512:1024], ropeb[0:64, 0:512], waits=[("aS", a1)])
        v1 = emit("vector", lambda e, s_=stt: e.tensor_tensor(
            out=ebA[:, 0:512], in0=ropeb[:, 0:512], in1=cosb[:, s_ * 512:(s_ + 1) * 512], op=ALU.mult),
            inc="vS", waits=flat([("aS", a1)], ev["m1_free"]))
        p1 = emit("gpsimd", lambda e, s_=stt: e.tensor_tensor(
            out=ebA[:, 512:1024], in0=ropeb[:, 512:1024], in1=sinb[:, s_ * 512:(s_ + 1) * 512], op=ALU.mult),
            inc="pS", waits=flat([("rp", c2)], ev["m2_free"]))
        v2 = emit("vector", lambda e, d=dst_bf: e.tensor_tensor(
            out=d, in0=ebA[:, 0:512], in1=ebA[:, 512:1024], op=ALU.add),
            inc="vS", waits=[("vS", v1), ("pS", p1)])
        ev["ropeb_free"] = [("vS", v2)]
        ev["m1_free"] = [("vS", v2)]
        ev["m2_free"] = [("vS", v2)]
        return [("aS", a1)], [("vS", v2)]

    probs_free = [[]]
    scor_free = [[]]

    def softmax_transpose(nkt, J, p, is_a, qoff, koff, pt_off, pr_off):
        i = 4 * J + p
        qsl = arena[:, qoff + i * 128: qoff + (i + 1) * 128]
        tlast = None
        for kt in range(nkt):
            tlast = emit("tensor", lambda e, q=qsl, kt_=kt, ko=koff: e.matmul(
                out=scor[:, kt_ * 512:(kt_ + 1) * 512], lhsT=q,
                rhs=arena[:, ko + kt_ * 512: ko + (kt_ + 1) * 512], start=True, stop=True),
                inc="tS" if kt == nkt - 1 else None,
                waits=flat(scor_free[0], ev["qkv"]) if kt == 0 else ())
        L = nkt * 512
        v1 = emit("vector", lambda e, L_=L: e.reduce_max(out=cols[:, 7:8], in_=scor[:, 0:L_], axis=AX.X),
                  inc="vS", waits=[("tS", tlast)])
        v2 = emit("vector", lambda e: e.tensor_scalar(out=cols[:, 8:9], in0=cols[:, 7:8],
                                                      scalar1=-SCALE, scalar2=None, op0=ALU.mult),
                  inc="vS", waits=[("vS", v1)])
        a1 = emit("scalar", lambda e, L_=L, po=pr_off: e.activation(
            out=arena[:, po:po + L_], in_=scor[:, 0:L_], func=AF.Exp,
            bias=cols[:, 8:9], scale=SCALE, accum_out=cols[:, 9:10]),
            inc="aS", waits=flat([("vS", v2)], probs_free[0]))
        scor_free[0] = [("aS", a1)]
        if is_a:
            pm = emit("gpsimd", lambda e, J_=J, p_=p, po=pr_off: e.tensor_tensor(
                out=arena[:, po + J_ * 512: po + (J_ + 1) * 512],
                in0=arena[:, po + J_ * 512: po + (J_ + 1) * 512],
                in1=mask01[:, p_ * 512:(p_ + 1) * 512], op=ALU.mult),
                inc="pS", waits=[("aS", a1)])
            v3 = emit("vector", lambda e, L_=L, po=pr_off: e.reduce_sum(
                out=cols[:, 9:10], in_=arena[:, po:po + L_], axis=AX.X),
                inc="vS", waits=[("pS", pm)])
            v4 = emit("vector", lambda e: e.reciprocal(out=cols[:, 10:11], in_=cols[:, 9:10]),
                      inc="vS", waits=[("vS", v3)])
        else:
            v4 = emit("vector", lambda e: e.reciprocal(out=cols[:, 10:11], in_=cols[:, 9:10]),
                      inc="vS", waits=[("aS", a1)])
        pn = emit("vector", lambda e, L_=L, po=pr_off: e.tensor_scalar(
            out=arena[:, po:po + L_], in0=arena[:, po:po + L_],
            scalar1=cols[:, 10:11], scalar2=None, op0=ALU.mult),
            inc="vS", waits=[("vS", v4)])
        tl_v = None
        for g4 in range(nkt):
            t = None
            for j in range(4):
                kc = g4 * 4 + j
                t = emit("tensor", lambda e, kk=kc, jj=j, po=pr_off: e.transpose(
                    out=ptp[:, jj * 128:(jj + 1) * 128],
                    in_=arena[:, po + kk * 128: po + (kk + 1) * 128], identity=ident),
                    inc="tS" if j == 3 else None,
                    waits=flat([("vS", pn)], ev["ptp_free"]) if j == 0 else ())
            src = ptp.rearrange("a (b c) -> a b c", b=4)
            dstbase = pt_off + g4 * 4 * 512 + p * 128
            dst = arena[:, dstbase: dstbase + 2048].rearrange("a (b c) -> a b c", b=4)[:, :, 0:128]
            tl_v = emit("vector", lambda e, sr=src, dt_=dst: e.tensor_copy(out=dt_, in_=sr),
                        inc="vS", waits=[("tS", t)])
            ev["ptp_free"] = [("vS", tl_v)]
        probs_free[0] = [("vS", tl_v)]

    def attn_a():
        ev["m1_free"] = flat(ev["ebA_free"])
        ev["m2_free"] = flat(ev["ebA_free"])
        for g in range(4):
            qkv_v = []
            for (wd_, kind) in ((wqtb_d, "q"), (wktb_d, "k"), (wvtb_d, "v")):
                wrdy = load_blk_all(wd_[g, :, :], 16 * 512, extra=pcw[f"qkv{g}"])
                if kind in ("q", "k"):
                    base = A_Q4 if kind == "q" else A_K4
                    for hh in range(4):
                        for stt in range(NT):
                            pp = next_psum()
                            tl = proj_accum(pp,
                                            [wbf[:, kc * 512 + hh * 128: kc * 512 + hh * 128 + 128] for kc in range(NCH)],
                                            [xnf[:, kc * S + stt * 512: kc * S + (stt + 1) * 512] for kc in range(NCH)],
                                            extra_waits=flat(wrdy, ev["xnf"]))
                            fe, re = rope_evac(pp, stt,
                                               arena[:, base + hh * S + stt * 512: base + hh * S + (stt + 1) * 512], tl)
                            psum_free[id(pp)] = fe
                            qkv_v.append(re[0][1])
                else:
                    for ssub in range(NS):
                        pp = next_psum()
                        tl = proj_accum(pp,
                                        [xnf[:, kc * S + ssub * 128: kc * S + ssub * 128 + 128] for kc in range(NCH)],
                                        [wbf[:, kc * 512:(kc + 1) * 512] for kc in range(NCH)],
                                        extra_waits=flat(wrdy, ev["xnf"]))
                        vv = emit("vector", lambda e, pp_=pp, s_=ssub: e.tensor_copy(
                            out=arena[:, A_V4 + s_ * 512:A_V4 + (s_ + 1) * 512], in_=pp_[:, 0:512]),
                            inc="vS", waits=[("tS", tl)])
                        psum_free[id(pp)] = [("vS", vv)]
                        qkv_v.append(vv)
                for kc in range(NCH):
                    wbf_free[kc] = [("tS", ctr["tS"])]
            ev["qkv"] = [("vS", max(qkv_v))]
            for hh in range(4):
                h = 4 * g + hh
                for J in range(4):
                    for p in range(4):
                        softmax_transpose(J + 1, J, p, True, A_Q4 + hh * S, A_K4 + hh * S, A_PT, A_PR)
                    nkc = (J + 1) * 4
                    pp = next_psum()
                    tl = proj_accum(pp,
                                    [arena[:, A_V4 + kc * 512 + hh * 128: A_V4 + kc * 512 + hh * 128 + 128] for kc in range(nkc)],
                                    [arena[:, A_PT + kc * 512: A_PT + (kc + 1) * 512] for kc in range(nkc)],
                                    extra_waits=flat(probs_free[0], ev["ctxs_free"]))
                    vv = emit("vector", lambda e, pp_=pp, J_=J: e.tensor_copy(
                        out=arena[:, A_CS + J_ * 512: A_CS + (J_ + 1) * 512], in_=pp_[:, 0:512]),
                        inc="vS", waits=[("tS", tl)])
                    psum_free[id(pp)] = [("vS", vv)]
                cw = None
                for sbq in range(4):
                    cw = dma("gpsimd", "cw", ctxa2_d[sbq, :, h * 512:(h + 1) * 512],
                             arena[:, A_CS + sbq * 512: A_CS + (sbq + 1) * 512],
                             waits=[("vS", ctr["vS"])])
                ev["ctxs_free"] = [("cw", cw)]
        ev["ebA_free"] = flat(ev["m1_free"], ev["m2_free"])

    ev["ctxs_free"] = []

    stage_free = [[] for _ in range(4)]

    def evac_stash(s_glob, ot, psum_ap, mode, wts):
        slot = (s_glob + ot) % 4
        stg = ebB[:, slot * 512:(slot + 1) * 512]
        w2 = flat(wts, stage_free[slot])
        if mode == "plain":
            vv = emit("vector", lambda e, p=psum_ap, o=stg: e.tensor_copy(out=o, in_=p),
                      inc="vS", waits=w2)
        elif mode == "bias":
            vv = emit("vector", lambda e, p=psum_ap, o=stg, t=ot: e.tensor_tensor(
                out=o, in0=p, in1=ebA[:, t * 512:(t + 1) * 512], op=ALU.add), inc="vS", waits=w2)
        else:
            vv = emit("vector", lambda e, p=psum_ap, o=stg, t=ot: e.tensor_tensor(
                out=o, in0=p, in1=ebA[:, t * 512:(t + 1) * 512], op=ALU.mult), inc="vS", waits=w2)
        cw = dma("gpsimd", f"sg{slot}", stash_d[s_glob * 128:(s_glob + 1) * 128, ot * 512:(ot + 1) * 512],
                 stg, waits=[("vS", vv)])
        stage_free[slot] = [(f"sg{slot}", cw)]

    def run_outproj(NK, mode, movb_d, src_d, src_ready, bias_d, wready):
        fence_wbf()
        if mode in ("bias", "sdown"):
            cA = dma("sync", "eA", ebA, bias_d[:, :], waits=flat(ev["ebA_free"]))
            ev["ebA_bias"] = [("eA", cA)]
        for ot in range(4):
            for sbi in range(4):
                sb = [sbi * 4 + j for j in range(4)]
                for kb in range(0, NK, 8):
                    half = (kb // 8) % 2
                    wrdy = ring_half_load(half, movb_d[ot, :, kb * 512:(kb + 8) * 512],
                                          extra=wready)
                    srdy = stat_half_load(half, src_d[sbi, :, kb * 512:(kb + 8) * 512],
                                          extra=src_ready)
                    for k in range(kb, kb + 8):
                        jw = k % 16 if False else (half * 8 + (k - kb))
                        for j in range(4):
                            w_ = flat(wrdy, srdy, scor_free[0] if (k == kb and j == 0) else [])
                            emit("tensor", lambda e, j_=j, jw_=jw, k_=k, NK_=NK: e.matmul(
                                out=scor[:, j_ * 512:(j_ + 1) * 512],
                                lhsT=arena[:, OP_CT + jw_ * 512 + j_ * 128: OP_CT + jw_ * 512 + j_ * 128 + 128],
                                rhs=wbf[:, jw_ * 512:(jw_ + 1) * 512],
                                start=(k_ == 0), stop=(k_ == NK_ - 1)),
                                inc="tS" if j == 3 else None,
                                waits=w_ if j == 0 else ())
                    ring_free[half] = [("tS", ctr["tS"])]
                    so_free[half] = [("tS", ctr["tS"])]
                tl = ctr["tS"]
                for j, s_g in enumerate(sb):
                    wts = [("tS", tl)]
                    if mode in ("bias", "sdown"):
                        wts = flat(wts, ev["ebA_bias"])
                    evac_stash(s_g, ot, scor[:, j * 512:(j + 1) * 512], mode, wts)
                scor_free[0] = [("vS", ctr["vS"])]
        ev["ebA_free"] = [("vS", ctr["vS"])]
        ev["ebB_free"] = [(f"sg{j}", ctr[f"sg{j}"]) for j in range(4)]

    def epilogue(sub):
        src = x_d if sub == 0 else xres_d
        dst = out_d if sub == 3 else xres_d
        wsem = "xw0" if sub % 2 == 0 else "xw1"
        rsem = "xw1" if sub % 2 == 0 else "xw0"
        stash_done = [(f"sg{j}", ctr[f"sg{j}"]) for j in range(4)]
        src_done = [] if sub == 0 else [(rsem, ctr[rsem])]
        last = None
        for s in range(NS):
            cA = dma("sync", "eA", ebA, stash_d[s * 128:(s + 1) * 128, :],
                     waits=flat(ev["ebA_free"], stash_done))
            cB = dma("sync", "eB", ebB, src[s * 128:(s + 1) * 128, :],
                     waits=flat(ev["ebB_free"], src_done))
            p1 = emit("gpsimd", lambda e: e.tensor_tensor(out=ebA, in0=ebA, in1=ebB, op=ALU.add),
                      inc="pS", waits=[("eA", cA), ("eB", cB)])
            ev["ebB_free"] = [("pS", p1)]
            cw = dma("gpsimd", wsem, dst[s * 128:(s + 1) * 128, :], ebA, waits=[("pS", p1)])
            last = cw
            if sub < 3:
                ln_chain(s, [("pS", p1)], ebA)
                ev["ebA_free"] = [("pS", ctr["pS"]), (wsem, cw)]
            else:
                ev["ebA_free"] = [(wsem, cw)]
        if sub < 3:
            ev["xnf"] = [("vS", ctr["vS"])]
        return last

    def ffn():
        fence_wbf()
        for sg in range(4):
            for icb in range(16):
                for kb in range(0, NCH, 8):
                    half = (kb // 8) % 2
                    wrdy = ring_half_load(half, wgb_d[icb, :, kb * 512:(kb + 8) * 512],
                                          extra=pcw["ffn"])
                    for kc in range(kb, kb + 8):
                        jw = half * 8 + (kc - kb)
                        for j in range(4):
                            w_ = flat(wrdy, scor_free[0] if (kc == kb and j == 0) else [],
                                      ev["xnf"] if (kc == kb and j == 0) else [])
                            emit("tensor", lambda e, j_=j, jw_=jw, kc_=kc, sg_=sg: e.matmul(
                                out=scor[:, j_ * 512:(j_ + 1) * 512],
                                lhsT=wbf[:, jw_ * 512 + j_ * 128: jw_ * 512 + j_ * 128 + 128],
                                rhs=xnf[:, kc_ * S + sg_ * 512: kc_ * S + (sg_ + 1) * 512],
                                start=(kc_ == 0), stop=(kc_ == NCH - 1)),
                                inc="tS" if j == 3 else None,
                                waits=w_ if j == 0 else ())
                    ring_free[half] = [("tS", ctr["tS"])]
                gate_stop = ctr["tS"]
                silu_ev = []
                for j in range(4):
                    ic = icb * 4 + j
                    aa = emit("scalar", lambda e, j_=j, ic_=ic: e.activation(
                        out=ebB[:, j_ * 512:(j_ + 1) * 512], in_=scor[:, j_ * 512:(j_ + 1) * 512],
                        func=AF.Silu, scale=sgc[:, ic_:ic_ + 1]),
                        inc="aS", waits=flat([("tS", gate_stop)], ev["ebB_free"] if j == 0 else []))
                    silu_ev.append(aa)
                scor_free[0] = [("aS", silu_ev[-1])]
                for kb in range(0, NCH, 8):
                    half = (kb // 8) % 2
                    wrdy = ring_half_load(half, wub_d[icb, :, kb * 512:(kb + 8) * 512])
                    for kc in range(kb, kb + 8):
                        jw = half * 8 + (kc - kb)
                        for j in range(4):
                            w_ = flat(wrdy, scor_free[0] if (kc == kb and j == 0) else [])
                            emit("tensor", lambda e, j_=j, jw_=jw, kc_=kc, sg_=sg: e.matmul(
                                out=scor[:, j_ * 512:(j_ + 1) * 512],
                                lhsT=wbf[:, jw_ * 512 + j_ * 128: jw_ * 512 + j_ * 128 + 128],
                                rhs=xnf[:, kc_ * S + sg_ * 512: kc_ * S + (sg_ + 1) * 512],
                                start=(kc_ == 0), stop=(kc_ == NCH - 1)),
                                inc="tS" if j == 3 else None,
                                waits=w_ if j == 0 else ())
                    ring_free[half] = [("tS", ctr["tS"])]
                up_stop = ctr["tS"]
                hv = []
                for j in range(4):
                    ic = icb * 4 + j
                    vv = emit("vector", lambda e, j_=j, ic_=ic: e.tensor_scalar(
                        out=ebA[:, j_ * 512:(j_ + 1) * 512], in0=scor[:, j_ * 512:(j_ + 1) * 512],
                        scalar1=suc[:, ic_:ic_ + 1], scalar2=None, op0=ALU.mult),
                        inc="vS", waits=flat([("tS", up_stop)], ev["ebA_free"] if j == 0 else []))
                    pp = emit("gpsimd", lambda e, j_=j, ic_=ic: e.tensor_tensor(
                        out=arena[:, F_HT + ic_ * 512: F_HT + (ic_ + 1) * 512],
                        in0=ebB[:, j_ * 512:(j_ + 1) * 512], in1=ebA[:, j_ * 512:(j_ + 1) * 512], op=ALU.mult),
                        inc="pS", waits=[("aS", silu_ev[j]), ("vS", vv)])
                    hv.append(pp)
                scor_free[0] = [("vS", ctr["vS"])]
                ev["ebB_free"] = [("pS", hv[-1])]
                ev["ebA_free"] = [("pS", hv[-1])]
            ev["hT"] = [("pS", ctr["pS"])]
            cA = dma("sync", "eA", ebA, sdbc_d[:, :], waits=flat(ev["ebA_free"]))
            ev["ebA_bias"] = [("eA", cA)]
            sb = [sg * 4 + j for j in range(4)]
            for ot in range(4):
                for kb in range(0, NIC, 8):
                    half = (kb // 8) % 2
                    wrdy = ring_half_load(half, wdb_d[ot, :, kb * 512:(kb + 8) * 512])
                    for k in range(kb, kb + 8):
                        jw = half * 8 + (k - kb)
                        for j in range(4):
                            w_ = flat(wrdy, ev["hT"],
                                      scor_free[0] if (k == kb and j == 0) else [])
                            emit("tensor", lambda e, j_=j, jw_=jw, k_=k: e.matmul(
                                out=scor[:, j_ * 512:(j_ + 1) * 512],
                                lhsT=arena[:, F_HT + k_ * 512 + j_ * 128: F_HT + k_ * 512 + j_ * 128 + 128],
                                rhs=wbf[:, jw_ * 512:(jw_ + 1) * 512],
                                start=(k_ == 0), stop=(k_ == NIC - 1)),
                                inc="tS" if j == 3 else None,
                                waits=w_ if j == 0 else ())
                    ring_free[half] = [("tS", ctr["tS"])]
                tl = ctr["tS"]
                for j, s_g in enumerate(sb):
                    evac_stash(s_g, ot, scor[:, j * 512:(j + 1) * 512], "sdown",
                               flat([("tS", tl)], ev["ebA_bias"]))
                scor_free[0] = [("vS", ctr["vS"])]
            ev["ebA_free"] = [("vS", ctr["vS"])]
            ev["ebB_free"] = [(f"sg{j}", ctr[f"sg{j}"]) for j in range(4)]

    def attn_b():
        fence_wbf()
        ev["m1_free"] = flat(ev["ebA_free"])
        ev["m2_free"] = flat(ev["ebA_free"])
        for kk in range(4):
            qkv_v = []
            wrdy = load_blk_all(wabq_d[kk, :, :], 16 * 512, extra=pcw["wattn"])
            for hh in range(4):
                for stt in range(NT):
                    pp = next_psum()
                    tl = proj_accum(pp,
                                    [wbf[:, kc * 512 + hh * 128: kc * 512 + hh * 128 + 128] for kc in range(NCH)],
                                    [xnf[:, kc * S + stt * 512: kc * S + (stt + 1) * 512] for kc in range(NCH)],
                                    extra_waits=flat(wrdy, ev["xnf"]))
                    fe, re = rope_evac(pp, stt, arena[:, B_Q4 + hh * S + stt * 512: B_Q4 + hh * S + (stt + 1) * 512], tl)
                    psum_free[id(pp)] = fe
                    qkv_v.append(re[0][1])
            for kc in range(NCH):
                wbf_free[kc] = [("tS", ctr["tS"])]
            wrdy = load_blk_all(wabk_d[kk, :, :], 16 * 128)
            for stt in range(NT):
                pp = next_psum()
                tl = proj_accum(pp,
                                [wbf[:, kc * 128: kc * 128 + 128] for kc in range(NCH)],
                                [xnf[:, kc * S + stt * 512: kc * S + (stt + 1) * 512] for kc in range(NCH)],
                                extra_waits=wrdy)
                fe, re = rope_evac(pp, stt, arena[:, B_KB + stt * 512: B_KB + (stt + 1) * 512], tl)
                psum_free[id(pp)] = fe
                qkv_v.append(re[0][1])
            for kc in range(NCH):
                wbf_free[kc] = [("tS", ctr["tS"])]
            wrdy = load_blk_all(wabv_d[kk, :, :], 16 * 512)
            for ssub in range(NS):
                pp = next_psum()
                tl = proj_accum(pp,
                                [xnf[:, kc * S + ssub * 128: kc * S + ssub * 128 + 128] for kc in range(NCH)],
                                [wbf[:, kc * 512:(kc + 1) * 512] for kc in range(NCH)],
                                extra_waits=wrdy)
                vv = emit("vector", lambda e, pp_=pp, s_=ssub: e.tensor_copy(
                    out=arena[:, B_V4 + s_ * 512: B_V4 + (s_ + 1) * 512], in_=pp_[:, 0:512]),
                    inc="vS", waits=[("tS", tl)])
                psum_free[id(pp)] = [("vS", vv)]
                qkv_v.append(vv)
            for kc in range(NCH):
                wbf_free[kc] = [("tS", ctr["tS"])]
            ev["qkv"] = [("vS", max(qkv_v))]
            for hh in range(4):
                h = 4 * kk + hh
                for J in range(4):
                    for p in range(4):
                        softmax_transpose(4, J, p, False, B_Q4 + hh * S, B_KB, B_PT, B_PR)
                    for dvc in range(4):
                        pp = next_psum()
                        tl = proj_accum(pp,
                                        [arena[:, B_V4 + kc * 512 + dvc * 128: B_V4 + kc * 512 + dvc * 128 + 128] for kc in range(16)],
                                        [arena[:, B_PT + kc * 512: B_PT + (kc + 1) * 512] for kc in range(16)],
                                        extra_waits=flat(probs_free[0], ev["ctxs_free"]))
                        vv = emit("vector", lambda e, pp_=pp, J_=J, d_=dvc: e.tensor_copy(
                            out=arena[:, B_CS + d_ * 2048 + J_ * 512: B_CS + d_ * 2048 + (J_ + 1) * 512],
                            in_=pp_[:, 0:512]),
                            inc="vS", waits=[("tS", tl)])
                        psum_free[id(pp)] = [("vS", vv)]
                cw = None
                for dvc in range(4):
                    cc = h * 4 + dvc
                    for sbq in range(4):
                        cw = dma("gpsimd", "cw", ctxb2_d[sbq, :, cc * 512:(cc + 1) * 512],
                                 arena[:, B_CS + dvc * 2048 + sbq * 512: B_CS + dvc * 2048 + (sbq + 1) * 512],
                                 waits=[("vS", ctr["vS"])])
                ev["ctxs_free"] = [("cw", cw)]
        ev["ebA_free"] = flat(ev["m1_free"], ev["m2_free"])

    attn_a()
    ctxa_done = [("cw", ctr["cw"])]
    run_outproj(16, "plain", wotb_d, ctxa2_d, ctxa_done, None, pcw["wot"])
    epilogue(0)
    ffn()
    epilogue(1)
    attn_b()
    ctxb_done = [("cw", ctr["cw"])]
    run_outproj(64, "bias", woutb_d, ctxb2_d, ctxb_done, bbc_d, pcw["wout"])
    epilogue(2)
    ffn()
    epilogue(3)

    emit("gpsimd", lambda e: e.memset(cols[:, 15:16], 0.0),
         waits=[(k, ctr[k]) for k in SEM_NAMES if ctr[k] > 0])

    with nc.Block() as block:
        def mk(ename):
            def sec(eng):
                for ws, fn, inc, n in streams[ename]:
                    for (s_, v) in ws:
                        eng.wait_ge(sems[s_], v)
                    inst = fn(eng)
                    if inc is not None:
                        inst.then_inc(sems[inc], n)
            sec.__name__ = ename
            return sec
        block.tensor(mk("tensor"))
        block.vector(mk("vector"))
        block.scalar(mk("scalar"))
        block.gpsimd(mk("gpsimd"))
        block.sync(mk("sync"))
    est.close()
    n_inst = sum(len(v) for v in streams.values())
    return nc, n_inst


def _host_constants():
    HD = 128
    TH = 10000.0
    inv = 1.0 / TH ** (np.arange(0, HD, 2) / HD)
    t = np.arange(S)
    ang_half = np.outer(inv, t)
    ang = np.concatenate([ang_half, ang_half], 0)
    cosb = np.cos(ang).astype(ml_dtypes.bfloat16)
    sin = np.sin(ang)
    sinS = np.concatenate([-sin[:64], sin[64:]], 0).astype(ml_dtypes.bfloat16)
    qq = np.arange(128)[:, None]
    m = np.zeros((128, 2048), dtype=np.float32)
    for p in range(4):
        kkk = np.arange(512)[None, :]
        m[:, p * 512:(p + 1) * 512] = (kkk <= p * 128 + qq).astype(np.float32)
    mask01 = m.astype(ml_dtypes.bfloat16)
    ident = np.eye(128, dtype=np.float32).astype(ml_dtypes.bfloat16)
    return cosb, sinS, mask01, ident


_CACHE = {}


def kernel(L0_attn_a_hidden_states, L0_attn_a_Wq, L0_attn_a_Wk, L0_attn_a_Wv, L0_attn_a_Wo,
           L0_attn_b_W_attn, L0_attn_b_W_out, L0_attn_b_b_out,
           L0_ffn_w_gate, L0_ffn_w_up, L0_ffn_w_down,
           L0_ffn_s_gate, L0_ffn_s_up, L0_ffn_s_down):
    x = np.asarray(L0_attn_a_hidden_states, dtype=np.float32)
    if "nc" not in _CACHE:
        _CACHE["nc"], _CACHE["n"] = build_program()
    nc = _CACHE["nc"]
    cosb, sinS, mask01, ident = _host_constants()
    wqt = np.ascontiguousarray(np.asarray(L0_attn_a_Wq, np.float32).T)
    wkt = np.ascontiguousarray(np.asarray(L0_attn_a_Wk, np.float32).T)
    wvt = np.ascontiguousarray(np.asarray(L0_attn_a_Wv, np.float32).T)
    wot = np.ascontiguousarray(np.asarray(L0_attn_a_Wo, np.float32).T)
    wattn = np.ascontiguousarray(np.asarray(L0_attn_b_W_attn, np.float32))
    wout = np.ascontiguousarray(np.asarray(L0_attn_b_W_out, np.float32))
    wg8 = np.ascontiguousarray(np.asarray(L0_ffn_w_gate, np.int8))
    wu8 = np.ascontiguousarray(np.asarray(L0_ffn_w_up, np.int8))
    wd8 = np.ascontiguousarray(np.asarray(L0_ffn_w_down, np.int8))
    sgc = np.ascontiguousarray(np.asarray(L0_ffn_s_gate, np.float32).reshape(64, 128).T)
    suc = np.ascontiguousarray(np.asarray(L0_ffn_s_up, np.float32).reshape(64, 128).T)
    sdbc = np.ascontiguousarray(np.broadcast_to(np.asarray(L0_ffn_s_down, np.float32)[None, :], (128, D)))
    bbc = np.ascontiguousarray(np.broadcast_to(np.asarray(L0_attn_b_b_out, np.float32)[None, :], (128, D)))
    shared = dict(wqt=wqt, wkt=wkt, wvt=wvt, wot=wot, wattn=wattn, wout=wout,
                  wg8=wg8, wu8=wu8, wd8=wd8, sgc=sgc, suc=suc, sdbc=sdbc, bbc=bbc,
                  cosb=cosb, sinb=sinS, mask01=mask01, ident=ident)
    in_maps = []
    for c in range(8):
        m = dict(shared)
        m["x"] = np.ascontiguousarray(x[c])
        in_maps.append(m)
    res = run_bass_kernel_spmd(nc, in_maps, core_ids=list(range(8)))
    return np.stack([np.asarray(r["out"]) for r in res.results], axis=0)


if __name__ == "__main__":
    nc, n = build_program()
    print("built", n, "instructions")


# version 06
# speedup vs baseline: 2.0233x; 1.0955x over previous
import math
import numpy as np
import ml_dtypes
from contextlib import ExitStack
import concourse.bass as bass
import concourse.mybir as mybir
from concourse.bass_utils import run_bass_kernel_spmd

F32 = mybir.dt.float32
BF16 = mybir.dt.bfloat16
I8 = mybir.dt.int8
AF = mybir.ActivationFunctionType
ALU = mybir.AluOpType
AX = mybir.AxisListType

S = 2048
D = 2048
NCH = 16
NS = 16
NT = 4
INTER = 8192
NIC = 64
SCALE = 1.0 / math.sqrt(128.0)
EPS = 1e-5

ARENA_COLS = 36864
A_Q4, A_K4, A_V4, A_PT, A_PR, A_CS = 0, 8192, 16384, 24576, 32768, 34816
B_Q4, B_KB, B_V4, B_PT, B_PR, B_CS = 0, 8192, 10240, 18432, 26624, 28672
F_HT = 0
XN_TM = 32768
OP_CT = 0


def build_program():
    nc = bass.Bass("TRN2", target_bir_lowering=False)

    def din(name, shape, dt):
        return nc.dram_tensor(name, shape, dt, kind="ExternalInput").ap()

    x_d = din("x", [S, D], F32)
    wqt_d = din("wqt", [D, D], F32)
    wkt_d = din("wkt", [D, D], F32)
    wvt_d = din("wvt", [D, D], F32)
    wot_d = din("wot", [D, D], F32)
    wattn_d = din("wattn", [D, 4608], F32)
    wout_d = din("wout", [INTER, D], F32)
    wg8_d = din("wg8", [D, INTER], I8)
    wu8_d = din("wu8", [D, INTER], I8)
    wd8_d = din("wd8", [INTER, D], I8)
    sgc_d = din("sgc", [128, 64], F32)
    suc_d = din("suc", [128, 64], F32)
    sdbc_d = din("sdbc", [128, D], F32)
    bbc_d = din("bbc", [128, D], F32)
    cosb_d = din("cosb", [128, S], BF16)
    sinb_d = din("sinb", [128, S], BF16)
    mask_d = din("mask01", [128, D], BF16)
    id_d = din("ident", [128, 128], BF16)

    out_d = nc.dram_tensor("out", [S, D], F32, kind="ExternalOutput").ap()
    xres_d = nc.dram_tensor("xres", [S, D], F32, kind="Internal").ap()
    stash_d = nc.dram_tensor("stash", [S, D], F32, kind="Internal").ap()
    wqtb_d = nc.dram_tensor("wqtb", [4, 128, 16 * 512], BF16, kind="Internal").ap()
    wktb_d = nc.dram_tensor("wktb", [4, 128, 16 * 512], BF16, kind="Internal").ap()
    wvtb_d = nc.dram_tensor("wvtb", [4, 128, 16 * 512], BF16, kind="Internal").ap()
    wotb_d = nc.dram_tensor("wotb", [4, 128, 16 * 512], BF16, kind="Internal").ap()
    wabq_d = nc.dram_tensor("wabq", [4, 128, 16 * 512], BF16, kind="Internal").ap()
    wabk_d = nc.dram_tensor("wabk", [4, 128, 16 * 128], BF16, kind="Internal").ap()
    wabv_d = nc.dram_tensor("wabv", [4, 128, 16 * 512], BF16, kind="Internal").ap()
    woutb_d = nc.dram_tensor("woutb", [4, 128, 64 * 512], BF16, kind="Internal").ap()
    wgb_d = nc.dram_tensor("wgb", [16, 128, 16 * 512], BF16, kind="Internal").ap()
    wub_d = nc.dram_tensor("wub", [16, 128, 16 * 512], BF16, kind="Internal").ap()
    wdb_d = nc.dram_tensor("wdb", [4, 128, 64 * 512], BF16, kind="Internal").ap()
    ctxa2_d = nc.dram_tensor("ctxa2", [4, 128, 16 * 512], BF16, kind="Internal").ap()
    ctxb2_d = nc.dram_tensor("ctxb2", [4, 128, 64 * 512], BF16, kind="Internal").ap()

    est = ExitStack()
    sems = {}
    SEM_NAMES = (["cN", "eA", "eB", "wb", "rp", "cw", "xw0", "xw1"]
                 + [f"pf{j}" for j in range(2)]
                 + [f"sg{j}" for j in range(4)]
                 + [f"wi{j}" for j in range(8)]
                 + [f"r{j}" for j in range(8)]
                 + [f"so{j}" for j in range(8)]
                 + [f"pc{j}" for j in range(4)]
                 + ["tS", "vS", "aS", "pS"])
    for k in SEM_NAMES:
        sems[k] = est.enter_context(nc.semaphore(k))
    DMA_SEMS = set(SEM_NAMES) - {"tS", "vS", "aS", "pS"}

    xnf = nc.alloc_sbuf_tensor("xnf", [128, NCH * S], BF16).ap()
    cosb = nc.alloc_sbuf_tensor("sb_cosb", [128, S], BF16).ap()
    sinb = nc.alloc_sbuf_tensor("sb_sinb", [128, S], BF16).ap()
    ident = nc.alloc_sbuf_tensor("sb_ident", [128, 128], BF16).ap()
    mask01 = nc.alloc_sbuf_tensor("sb_mask01", [128, D], BF16).ap()
    sgc = nc.alloc_sbuf_tensor("sb_sgc", [128, 64], F32).ap()
    suc = nc.alloc_sbuf_tensor("sb_suc", [128, 64], F32).ap()
    cols = nc.alloc_sbuf_tensor("cols", [128, 16], F32).ap()
    arena = nc.alloc_sbuf_tensor("arena", [128, ARENA_COLS], BF16).ap()
    ebA = nc.alloc_sbuf_tensor("ebA", [128, D], F32).ap()
    ebB = nc.alloc_sbuf_tensor("ebB", [128, D], F32).ap()
    ropeb = nc.alloc_sbuf_tensor("ropeb", [128, 2048], BF16).ap()
    wbf = nc.alloc_sbuf_tensor("wbf", [128, 16 * 512], BF16).ap()
    wsti = nc.alloc_sbuf_tensor("wsti", [128, 8 * 512], I8).ap()
    pcv = nc.alloc_sbuf_tensor("pcv", [128, 4 * 512], BF16).ap()
    pcf = nc.alloc_sbuf_tensor("pcf", [128, 2 * 512], F32).ap()
    bbuf = nc.alloc_sbuf_tensor("bbuf", [128, D], F32).ap()

    scor = nc.alloc_psum_tensor("scor", [128, 2048], F32).ap()
    pA = nc.alloc_psum_tensor("pA", [128, 512], F32).ap()
    pB = nc.alloc_psum_tensor("pB", [128, 512], F32).ap()
    pC = nc.alloc_psum_tensor("pC", [128, 512], F32).ap()
    ptp = nc.alloc_psum_tensor("ptp", [128, 512], BF16).ap()

    streams = {e: [] for e in ("tensor", "vector", "scalar", "gpsimd", "sync")}
    ctr = {k: 0 for k in sems}
    water = {e: {k: 0 for k in sems} for e in streams}

    def flat(*evs):
        out = []
        for e in evs:
            if e is None:
                continue
            if isinstance(e, list):
                out.extend(x for x in e if x is not None)
            else:
                out.append(e)
        return out

    def emit(eng, fn, inc=None, waits=()):
        byname = {}
        for (s_, v) in waits:
            if v > byname.get(s_, 0):
                byname[s_] = v
        ws = []
        for s_, v in byname.items():
            if v > water[eng][s_]:
                ws.append((s_, v))
                water[eng][s_] = v
        n = 0
        if inc is not None:
            n = 16 if inc in DMA_SEMS else 1
            ctr[inc] += n
        streams[eng].append((ws, fn, inc, n))
        return ctr[inc] if inc else None

    def dma(q, sem, out, in_, waits=()):
        return emit(q, lambda e, o=out, i=in_: e.dma_start(out=o, in_=i), inc=sem, waits=waits)

    ev = {
        "xnf": [], "ebA_free": [], "ebB_free": [], "ptp_free": [], "ln_free": [],
        "ropeb_free": [], "m1_free": [], "m2_free": [], "qkv": [], "ctxs_free": [],
        "consts": [], "hT": [], "ebA_bias": [], "bbuf_free": [],
    }

    pcw = {}
    _pcf_free = [[], []]
    _pcf_next = [0]
    _wsti_free = [[] for _ in range(8)]
    _wsti_next = [0]
    _pcv_free = [[] for _ in range(4)]
    _pcv_next = [0]

    def _pcv_slot():
        j = _pcv_next[0]; _pcv_next[0] = (j + 1) % 4
        return j

    def preconv_f32(src_tile, dst_tile, w=512):
        j = _pcf_next[0]; _pcf_next[0] = (j + 1) % 2
        c = dma("scalar", f"pf{j}", pcf[:, j * 512:j * 512 + w], src_tile, waits=_pcf_free[j])
        v = _pcv_slot()
        aa = emit("scalar", lambda e, jj=j, vv=v, w_=w: e.copy(out=pcv[:, vv * 512:vv * 512 + w_],
                                                               in_=pcf[:, jj * 512:jj * 512 + w_]),
                  inc="aS", waits=flat([(f"pf{j}", c)], _pcv_free[v]))
        _pcf_free[j] = [("aS", aa)]
        cw = dma("gpsimd", f"pc{v}", dst_tile, pcv[:, v * 512:v * 512 + w], waits=[("aS", aa)])
        _pcv_free[v] = [(f"pc{v}", cw)]

    def preconv_i8(src_tile, dst_tile):
        j = _wsti_next[0]; _wsti_next[0] = (j + 1) % 8
        c = dma("scalar", f"wi{j}", wsti[:, j * 512:(j + 1) * 512], src_tile, waits=_wsti_free[j])
        v = _pcv_slot()
        pp = emit("gpsimd", lambda e, jj=j, vv=v: e.tensor_copy(out=pcv[:, vv * 512:(vv + 1) * 512],
                                                                in_=wsti[:, jj * 512:(jj + 1) * 512]),
                  inc="pS", waits=flat([(f"wi{j}", c)], _pcv_free[v]))
        _wsti_free[j] = [("pS", pp)]
        cw = dma("gpsimd", f"pc{v}", dst_tile, pcv[:, v * 512:(v + 1) * 512], waits=[("pS", pp)])
        _pcv_free[v] = [(f"pc{v}", cw)]

    def pcw_mark(name):
        pcw[name] = [(f"pc{j}", ctr[f"pc{j}"]) for j in range(4)]

    PRE_TASKS = []

    def pump(n):
        while n > 0 and PRE_TASKS:
            t = PRE_TASKS.pop(0)
            if isinstance(t, str):
                pcw_mark(t)
            else:
                t()
                n -= 1

    def pump_until(name):
        while name not in pcw:
            t = PRE_TASKS.pop(0)
            if isinstance(t, str):
                pcw_mark(t)
            else:
                t()

    def emit_preconvert_qkv_weights():
        for g in range(4):
            for (srcd, dstd) in ((wqt_d, wqtb_d), (wkt_d, wktb_d), (wvt_d, wvtb_d)):
                for kc in range(NCH):
                    PRE_TASKS.append(lambda srcd=srcd, dstd=dstd, kc=kc, g=g: preconv_f32(
                        srcd[kc * 128:(kc + 1) * 128, g * 512:(g + 1) * 512],
                        dstd[g, :, kc * 512:(kc + 1) * 512]))
            PRE_TASKS.append(f"qkv{g}")

    def emit_preconvert_rest():
        A = PRE_TASKS.append
        for kc in range(NCH):
            for j in range(4):
                A(lambda kc=kc, j=j: preconv_f32(wot_d[kc * 128:(kc + 1) * 128, j * 512:(j + 1) * 512],
                                                 wotb_d[j, :, kc * 512:(kc + 1) * 512]))
        A("wot")
        for kc in range(NCH):
            for j in range(16):
                A(lambda kc=kc, j=j: preconv_i8(wg8_d[kc * 128:(kc + 1) * 128, j * 512:(j + 1) * 512],
                                                wgb_d[j, :, kc * 512:(kc + 1) * 512]))
                A(lambda kc=kc, j=j: preconv_i8(wu8_d[kc * 128:(kc + 1) * 128, j * 512:(j + 1) * 512],
                                                wub_d[j, :, kc * 512:(kc + 1) * 512]))
        for k in range(NIC):
            for j in range(4):
                A(lambda k=k, j=j: preconv_i8(wd8_d[k * 128:(k + 1) * 128, j * 512:(j + 1) * 512],
                                              wdb_d[j, :, k * 512:(k + 1) * 512]))
        A("ffn")
        for kk in range(4):
            for kc in range(NCH):
                A(lambda kk=kk, kc=kc: preconv_f32(wattn_d[kc * 128:(kc + 1) * 128, kk * 512:(kk + 1) * 512],
                                                   wabq_d[kk, :, kc * 512:(kc + 1) * 512]))
            for kc in range(NCH):
                A(lambda kk=kk, kc=kc: preconv_f32(
                    wattn_d[kc * 128:(kc + 1) * 128, 2048 + kk * 128: 2048 + (kk + 1) * 128],
                    wabk_d[kk, :, kc * 128:(kc + 1) * 128], w=128))
            for kc in range(NCH):
                A(lambda kk=kk, kc=kc: preconv_f32(
                    wattn_d[kc * 128:(kc + 1) * 128, 2560 + kk * 512: 2560 + (kk + 1) * 512],
                    wabv_d[kk, :, kc * 512:(kc + 1) * 512]))
        A("wattn")
        for k in range(NIC):
            for j in range(4):
                A(lambda k=k, j=j: preconv_f32(wout_d[k * 128:(k + 1) * 128, j * 512:(j + 1) * 512],
                                               woutb_d[j, :, k * 512:(k + 1) * 512]))
        A("wout")

    wbf_free = [[] for _ in range(16)]
    wbf_ready = [[] for _ in range(16)]

    def load_blk_all(dram_slice, ncols, extra=()):
        c = dma("sync", "wb", wbf[:, 0:ncols], dram_slice,
                waits=flat(*[wbf_free[x] for x in range(16)], list(extra)))
        for x in range(16):
            wbf_ready[x] = [("wb", c)]
        return [("wb", c)]

    ring_free = [[] for _ in range(2)]

    def ring_half_load(half, dram_slice, extra=()):
        c = dma("sync", f"r{half}", wbf[:, half * 4096:(half + 1) * 4096], dram_slice,
                waits=flat(ring_free[half], list(extra)))
        return [(f"r{half}", c)]

    so_free = [[], []]

    def fence_wbf():
        e_ = [("tS", ctr["tS"])]
        for j in range(16):
            wbf_free[j] = list(e_)
        for j in range(2):
            ring_free[j] = list(e_)
            so_free[j] = list(e_)

    def stat_half_load(half, dram_slice, extra=()):
        c = dma("sync", f"so{half}", arena[:, OP_CT + half * 4096: OP_CT + (half + 1) * 4096],
                dram_slice, waits=flat(so_free[half], list(extra)))
        return [(f"so{half}", c)]

    dma("sync", "cN", cosb, cosb_d[:, :])
    dma("sync", "cN", sinb, sinb_d[:, :])
    dma("sync", "cN", ident, id_d[:, :])
    dma("sync", "cN", mask01, mask_d[:, :])
    dma("sync", "cN", sgc, sgc_d[:, :])
    c = dma("sync", "cN", suc, suc_d[:, :])
    ev["consts"] = [("cN", c)]

    def ln_chain(s, pre_waits, buf):
        pw = flat(pre_waits, ev["ln_free"])
        v1 = emit("vector", lambda e, b=buf: e.reduce_sum(out=cols[:, 0:1], in_=b, axis=AX.X),
                  inc="vS", waits=pw)
        a1 = emit("scalar", lambda e, b=buf: e.activation(
            out=arena[:, XN_TM:XN_TM + 2048], in_=b, func=AF.Square, accum_out=cols[:, 1:2]),
            inc="aS", waits=pw)
        a2 = emit("scalar", lambda e: e.mul(out=cols[:, 2:3], in_=cols[:, 0:1], mul=1.0 / D),
                  inc="aS", waits=[("vS", v1)])
        v2 = emit("vector", lambda e: e.tensor_scalar(out=cols[:, 3:4], in0=cols[:, 2:3],
                                                      scalar1=cols[:, 2:3], scalar2=-EPS,
                                                      op0=ALU.mult, op1=ALU.add),
                  inc="vS", waits=[("aS", a2)])
        p1 = emit("gpsimd", lambda e: e.tensor_scalar(out=cols[:, 4:5], in0=cols[:, 1:2],
                                                      scalar1=1.0 / D, scalar2=cols[:, 3:4],
                                                      op0=ALU.mult, op1=ALU.subtract),
                  inc="pS", waits=[("aS", a1), ("vS", v2)])
        a3 = emit("scalar", lambda e: e.activation(out=cols[:, 5:6], in_=cols[:, 4:5],
                                                   func=AF.Sqrt),
                  inc="aS", waits=[("pS", p1)])
        v3 = emit("vector", lambda e: e.reciprocal(out=cols[:, 6:7], in_=cols[:, 5:6]),
                  inc="vS", waits=[("aS", a3)])
        p2 = emit("gpsimd", lambda e, b=buf: e.tensor_scalar(
            out=arena[:, XN_TM:XN_TM + 2048], in0=b,
            scalar1=cols[:, 2:3], scalar2=cols[:, 6:7], op0=ALU.subtract, op1=ALU.mult),
            inc="pS", waits=[("vS", v3), ("aS", a1)])
        last_t = 0
        xnfv = xnf.rearrange("a (b c) -> a b c", b=NCH)
        last_v = None
        for g4 in range(4):
            t = None
            for j in range(4):
                kc = g4 * 4 + j
                t = emit("tensor", lambda e, kk=kc, jj=j: e.transpose(
                    out=ptp[:, jj * 128:(jj + 1) * 128],
                    in_=arena[:, XN_TM + kk * 128:XN_TM + (kk + 1) * 128], identity=ident),
                    inc="tS" if j == 3 else None,
                    waits=flat([("pS", p2)], ev["ptp_free"], ev["consts"]) if j == 0 else ())
            last_t = t
            src = ptp.rearrange("a (b c) -> a b c", b=4)
            dst = xnfv[:, g4 * 4:g4 * 4 + 4, s * 128:(s + 1) * 128]
            last_v = emit("vector", lambda e, sr=src, dt_=dst: e.tensor_copy(out=dt_, in_=sr),
                          inc="vS", waits=[("tS", t)])
            ev["ptp_free"] = [("vS", last_v)]
        ev["ln_free"] = [("pS", p2), ("tS", last_t), ("vS", last_v)]
        return last_v

    emit_preconvert_qkv_weights()
    emit_preconvert_rest()
    xnf_last = 0
    for s in range(NS):
        buf, key, sm = (ebA, "ebA_free", "eA") if s % 2 == 0 else (ebB, "ebB_free", "eB")
        c = dma("sync", sm, buf, x_d[s * 128:(s + 1) * 128, :], waits=flat(ev[key], ev["consts"]))
        xnf_last = ln_chain(s, [(sm, c)], buf)
        ev[key] = [("pS", ctr["pS"])]
        pump(3)
    ev["xnf"] = [("vS", xnf_last)]

    psum_rot = [pA, pB, pC]
    psum_free = {id(pA): [], id(pB): [], id(pC): []}
    rot_i = [0]

    def next_psum():
        p = psum_rot[rot_i[0] % 3]
        rot_i[0] += 1
        return p

    def proj_accum(dst_psum, stat_slices, rhs_slices, extra_waits=()):
        n = len(stat_slices)
        tlast = None
        for k in range(n):
            tlast = emit("tensor", lambda e, a=stat_slices[k], b=rhs_slices[k], k_=k, n_=n,
                         dp=dst_psum: e.matmul(out=dp[:, 0:512], lhsT=a, rhs=b,
                                               start=(k_ == 0), stop=(k_ == n_ - 1)),
                         inc="tS" if k == n - 1 else None,
                         waits=flat(list(extra_waits), psum_free[id(dst_psum)]) if k == 0 else ())
        return tlast

    rope_par = [0]
    rope_free2 = [[], []]
    m_free2 = [[], []]

    def rope_evac(psum, stt, dst_bf, mm_done):
        P = rope_par[0]; rope_par[0] ^= 1
        rb = P * 1024
        mb = P * 1024
        a1 = emit("scalar", lambda e, p=psum, rb=rb: e.copy(out=ropeb[:, rb:rb + 512], in_=p[:, 0:512]),
                  inc="aS", waits=flat([("tS", mm_done)], rope_free2[P]))
        c1 = dma("gpsimd", "rp", ropeb[0:64, rb + 512:rb + 1024], ropeb[64:128, rb:rb + 512],
                 waits=[("aS", a1)])
        c2 = dma("gpsimd", "rp", ropeb[64:128, rb + 512:rb + 1024], ropeb[0:64, rb:rb + 512],
                 waits=[("aS", a1)])
        v1 = emit("vector", lambda e, s_=stt, rb=rb, mb=mb: e.tensor_tensor(
            out=ebA[:, mb:mb + 512], in0=ropeb[:, rb:rb + 512],
            in1=cosb[:, s_ * 512:(s_ + 1) * 512], op=ALU.mult),
            inc="vS", waits=flat([("aS", a1)], m_free2[P]))
        p1 = emit("gpsimd", lambda e, s_=stt, rb=rb, mb=mb: e.tensor_tensor(
            out=ebA[:, mb + 512:mb + 1024], in0=ropeb[:, rb + 512:rb + 1024],
            in1=sinb[:, s_ * 512:(s_ + 1) * 512], op=ALU.mult),
            inc="pS", waits=flat([("rp", c2)], m_free2[P]))
        v2 = emit("vector", lambda e, d=dst_bf, mb=mb: e.tensor_tensor(
            out=d, in0=ebA[:, mb:mb + 512], in1=ebA[:, mb + 512:mb + 1024], op=ALU.add),
            inc="vS", waits=[("vS", v1), ("pS", p1)])
        rope_free2[P] = [("vS", v2)]
        m_free2[P] = [("vS", v2)]
        return [("aS", a1)], [("vS", v2)]

    probs_free = [[]]
    scor_free = [[]]

    def softmax_transpose(nkt, J, p, is_a, qoff, koff, pt_off, pr_off):
        i = 4 * J + p
        qsl = arena[:, qoff + i * 128: qoff + (i + 1) * 128]
        tlast = None
        for kt in range(nkt):
            tlast = emit("tensor", lambda e, q=qsl, kt_=kt, ko=koff: e.matmul(
                out=scor[:, kt_ * 512:(kt_ + 1) * 512], lhsT=q,
                rhs=arena[:, ko + kt_ * 512: ko + (kt_ + 1) * 512], start=True, stop=True),
                inc="tS" if kt == nkt - 1 else None,
                waits=flat(scor_free[0], ev["qkv"]) if kt == 0 else ())
        L = nkt * 512
        v1 = emit("vector", lambda e, L_=L: e.reduce_max(out=cols[:, 7:8], in_=scor[:, 0:L_], axis=AX.X),
                  inc="vS", waits=[("tS", tlast)])
        v2 = emit("vector", lambda e: e.tensor_scalar(out=cols[:, 8:9], in0=cols[:, 7:8],
                                                      scalar1=-SCALE, scalar2=None, op0=ALU.mult),
                  inc="vS", waits=[("vS", v1)])
        a1 = emit("scalar", lambda e, L_=L, po=pr_off: e.activation(
            out=arena[:, po:po + L_], in_=scor[:, 0:L_], func=AF.Exp,
            bias=cols[:, 8:9], scale=SCALE, accum_out=cols[:, 9:10]),
            inc="aS", waits=flat([("vS", v2)], probs_free[0]))
        scor_free[0] = [("aS", a1)]
        if is_a:
            pm = emit("gpsimd", lambda e, J_=J, p_=p, po=pr_off: e.tensor_tensor(
                out=arena[:, po + J_ * 512: po + (J_ + 1) * 512],
                in0=arena[:, po + J_ * 512: po + (J_ + 1) * 512],
                in1=mask01[:, p_ * 512:(p_ + 1) * 512], op=ALU.mult),
                inc="pS", waits=[("aS", a1)])
            v3 = emit("vector", lambda e, L_=L, po=pr_off: e.reduce_sum(
                out=cols[:, 9:10], in_=arena[:, po:po + L_], axis=AX.X),
                inc="vS", waits=[("pS", pm)])
            v4 = emit("vector", lambda e: e.reciprocal(out=cols[:, 10:11], in_=cols[:, 9:10]),
                      inc="vS", waits=[("vS", v3)])
        else:
            v4 = emit("vector", lambda e: e.reciprocal(out=cols[:, 10:11], in_=cols[:, 9:10]),
                      inc="vS", waits=[("aS", a1)])
        pn = emit("vector", lambda e, L_=L, po=pr_off: e.tensor_scalar(
            out=arena[:, po:po + L_], in0=arena[:, po:po + L_],
            scalar1=cols[:, 10:11], scalar2=None, op0=ALU.mult),
            inc="vS", waits=[("vS", v4)])
        tl_v = None
        for g4 in range(nkt):
            t = None
            for j in range(4):
                kc = g4 * 4 + j
                t = emit("tensor", lambda e, kk=kc, jj=j, po=pr_off: e.transpose(
                    out=ptp[:, jj * 128:(jj + 1) * 128],
                    in_=arena[:, po + kk * 128: po + (kk + 1) * 128], identity=ident),
                    inc="tS" if j == 3 else None,
                    waits=flat([("vS", pn)], ev["ptp_free"]) if j == 0 else ())
            src = ptp.rearrange("a (b c) -> a b c", b=4)
            dstbase = pt_off + g4 * 4 * 512 + p * 128
            dst = arena[:, dstbase: dstbase + 2048].rearrange("a (b c) -> a b c", b=4)[:, :, 0:128]
            tl_v = emit("vector", lambda e, sr=src, dt_=dst: e.tensor_copy(out=dt_, in_=sr),
                        inc="vS", waits=[("tS", t)])
            ev["ptp_free"] = [("vS", tl_v)]
        probs_free[0] = [("vS", tl_v)]

    def attn_a():
        m_free2[0] = flat(ev["ebA_free"])
        m_free2[1] = flat(ev["ebA_free"])
        rope_free2[0] = []
        rope_free2[1] = []
        for g in range(4):
            pump_until(f"qkv{g}")
            qkv_v = []
            for (wd_, kind) in ((wqtb_d, "q"), (wktb_d, "k"), (wvtb_d, "v")):
                wrdy = load_blk_all(wd_[g, :, :], 16 * 512, extra=pcw[f"qkv{g}"])
                if kind in ("q", "k"):
                    base = A_Q4 if kind == "q" else A_K4
                    for hh in range(4):
                        for stt in range(NT):
                            pp = next_psum()
                            tl = proj_accum(pp,
                                            [wbf[:, kc * 512 + hh * 128: kc * 512 + hh * 128 + 128] for kc in range(NCH)],
                                            [xnf[:, kc * S + stt * 512: kc * S + (stt + 1) * 512] for kc in range(NCH)],
                                            extra_waits=flat(wrdy, ev["xnf"]))
                            fe, re = rope_evac(pp, stt,
                                               arena[:, base + hh * S + stt * 512: base + hh * S + (stt + 1) * 512], tl)
                            psum_free[id(pp)] = fe
                            qkv_v.append(re[0][1])
                else:
                    for ssub in range(NS):
                        pp = next_psum()
                        tl = proj_accum(pp,
                                        [xnf[:, kc * S + ssub * 128: kc * S + ssub * 128 + 128] for kc in range(NCH)],
                                        [wbf[:, kc * 512:(kc + 1) * 512] for kc in range(NCH)],
                                        extra_waits=flat(wrdy, ev["xnf"]))
                        vv = emit("vector", lambda e, pp_=pp, s_=ssub: e.tensor_copy(
                            out=arena[:, A_V4 + s_ * 512:A_V4 + (s_ + 1) * 512], in_=pp_[:, 0:512]),
                            inc="vS", waits=[("tS", tl)])
                        psum_free[id(pp)] = [("vS", vv)]
                        qkv_v.append(vv)
                for kc in range(NCH):
                    wbf_free[kc] = [("tS", ctr["tS"])]
            ev["qkv"] = [("vS", max(qkv_v))]
            for hh in range(4):
                h = 4 * g + hh
                for J in range(4):
                    pump(7)
                    for p in range(4):
                        softmax_transpose(J + 1, J, p, True, A_Q4 + hh * S, A_K4 + hh * S, A_PT, A_PR)
                    nkc = (J + 1) * 4
                    pp = next_psum()
                    tl = proj_accum(pp,
                                    [arena[:, A_V4 + kc * 512 + hh * 128: A_V4 + kc * 512 + hh * 128 + 128] for kc in range(nkc)],
                                    [arena[:, A_PT + kc * 512: A_PT + (kc + 1) * 512] for kc in range(nkc)],
                                    extra_waits=flat(probs_free[0], ev["ctxs_free"]))
                    vv = emit("vector", lambda e, pp_=pp, J_=J: e.tensor_copy(
                        out=arena[:, A_CS + J_ * 512: A_CS + (J_ + 1) * 512], in_=pp_[:, 0:512]),
                        inc="vS", waits=[("tS", tl)])
                    psum_free[id(pp)] = [("vS", vv)]
                cw = None
                for sbq in range(4):
                    cw = dma("gpsimd", "cw", ctxa2_d[sbq, :, h * 512:(h + 1) * 512],
                             arena[:, A_CS + sbq * 512: A_CS + (sbq + 1) * 512],
                             waits=[("vS", ctr["vS"])])
                ev["ctxs_free"] = [("cw", cw)]
        ev["ebA_free"] = flat(m_free2[0], m_free2[1])

    ev["ctxs_free"] = []

    stage_free = [[] for _ in range(4)]

    def evac_stash(s_glob, ot, psum_ap, mode, wts):
        slot = (s_glob + ot) % 4
        stg = ebB[:, slot * 512:(slot + 1) * 512]
        w2 = flat(wts, stage_free[slot], ev["ebB_free"])
        if mode == "plain":
            vv = emit("vector", lambda e, p=psum_ap, o=stg: e.tensor_copy(out=o, in_=p),
                      inc="vS", waits=w2)
        elif mode == "bias":
            vv = emit("vector", lambda e, p=psum_ap, o=stg, t=ot: e.tensor_tensor(
                out=o, in0=p, in1=bbuf[:, t * 512:(t + 1) * 512], op=ALU.add), inc="vS", waits=w2)
        else:
            vv = emit("vector", lambda e, p=psum_ap, o=stg, t=ot: e.tensor_tensor(
                out=o, in0=p, in1=bbuf[:, t * 512:(t + 1) * 512], op=ALU.mult), inc="vS", waits=w2)
        cw = dma("gpsimd", f"sg{slot}", stash_d[s_glob * 128:(s_glob + 1) * 128, ot * 512:(ot + 1) * 512],
                 stg, waits=[("vS", vv)])
        stage_free[slot] = [(f"sg{slot}", cw)]

    def run_outproj(NK, mode, movb_d, src_d, src_ready, bias_d, wready, sub):
        fence_wbf()
        epi_begin(sub)
        if mode in ("bias", "sdown"):
            cA = dma("sync", "eA", bbuf, bias_d[:, :], waits=flat(ev["bbuf_free"]))
            ev["ebA_bias"] = [("eA", cA)]
        for sbi in range(4):
            sb = [sbi * 4 + j for j in range(4)]
            for ot in range(4):
                for kb in range(0, NK, 8):
                    half = (kb // 8) % 2
                    wrdy = ring_half_load(half, movb_d[ot, :, kb * 512:(kb + 8) * 512],
                                          extra=wready)
                    srdy = stat_half_load(half, src_d[sbi, :, kb * 512:(kb + 8) * 512],
                                          extra=src_ready)
                    for k in range(kb, kb + 8):
                        jw = half * 8 + (k - kb)
                        for j in range(4):
                            w_ = flat(wrdy, srdy, scor_free[0] if (k == kb and j == 0) else [])
                            emit("tensor", lambda e, j_=j, jw_=jw, k_=k, NK_=NK: e.matmul(
                                out=scor[:, j_ * 512:(j_ + 1) * 512],
                                lhsT=arena[:, OP_CT + jw_ * 512 + j_ * 128: OP_CT + jw_ * 512 + j_ * 128 + 128],
                                rhs=wbf[:, jw_ * 512:(jw_ + 1) * 512],
                                start=(k_ == 0), stop=(k_ == NK_ - 1)),
                                inc="tS" if j == 3 else None,
                                waits=w_ if j == 0 else ())
                    ring_free[half] = [("tS", ctr["tS"])]
                    so_free[half] = [("tS", ctr["tS"])]
                tl = ctr["tS"]
                for j, s_g in enumerate(sb):
                    wts = [("tS", tl)]
                    if mode in ("bias", "sdown"):
                        wts = flat(wts, ev["ebA_bias"])
                    evac_stash(s_g, ot, scor[:, j * 512:(j + 1) * 512], mode, wts)
                scor_free[0] = [("vS", ctr["vS"])]
                pump(12)
            for s_g in sb:
                epi_chain(sub, s_g)
        ev["bbuf_free"] = [("vS", ctr["vS"])]
        if sub < 3:
            ev["xnf"] = [("vS", ctr["vS"])]

    epi_src_done = [[]]

    def epi_begin(sub):
        rsem = "xw1" if sub % 2 == 0 else "xw0"
        epi_src_done[0] = [] if sub == 0 else [(rsem, ctr[rsem])]

    def epi_chain(sub, s):
        srcd = x_d if sub == 0 else xres_d
        dstd = out_d if sub == 3 else xres_d
        wsem = "xw0" if sub % 2 == 0 else "xw1"
        stash_done = [(f"sg{j}", ctr[f"sg{j}"]) for j in range(4)]
        cA = dma("sync", "eA", ebA, stash_d[s * 128:(s + 1) * 128, :],
                 waits=flat(ev["ebA_free"], stash_done))
        cB = dma("sync", "eB", ebB, srcd[s * 128:(s + 1) * 128, :],
                 waits=flat(ev["ebB_free"], epi_src_done[0], stash_done))
        p1 = emit("gpsimd", lambda e: e.tensor_tensor(out=ebA, in0=ebA, in1=ebB, op=ALU.add),
                  inc="pS", waits=[("eA", cA), ("eB", cB)])
        ev["ebB_free"] = [("pS", p1)]
        cw = dma("gpsimd", wsem, dstd[s * 128:(s + 1) * 128, :], ebA, waits=[("pS", p1)])
        if sub < 3:
            ln_chain(s, [("pS", p1)], ebA)
            ev["ebA_free"] = [("pS", ctr["pS"]), (wsem, cw)]
        else:
            ev["ebA_free"] = [(wsem, cw)]

    def ffn(sub):
        fence_wbf()
        epi_begin(sub)
        for sg in range(4):
            for icb in range(16):
                for kb in range(0, NCH, 8):
                    half = (kb // 8) % 2
                    wrdy = ring_half_load(half, wgb_d[icb, :, kb * 512:(kb + 8) * 512],
                                          extra=pcw["ffn"])
                    for kc in range(kb, kb + 8):
                        jw = half * 8 + (kc - kb)
                        for j in range(4):
                            w_ = flat(wrdy, scor_free[0] if (kc == kb and j == 0) else [],
                                      ev["xnf"] if (kc == kb and j == 0) else [])
                            emit("tensor", lambda e, j_=j, jw_=jw, kc_=kc, sg_=sg: e.matmul(
                                out=scor[:, j_ * 512:(j_ + 1) * 512],
                                lhsT=wbf[:, jw_ * 512 + j_ * 128: jw_ * 512 + j_ * 128 + 128],
                                rhs=xnf[:, kc_ * S + sg_ * 512: kc_ * S + (sg_ + 1) * 512],
                                start=(kc_ == 0), stop=(kc_ == NCH - 1)),
                                inc="tS" if j == 3 else None,
                                waits=w_ if j == 0 else ())
                    ring_free[half] = [("tS", ctr["tS"])]
                gate_stop = ctr["tS"]
                pump(2)
                silu_ev = []
                for j in range(4):
                    ic = icb * 4 + j
                    aa = emit("scalar", lambda e, j_=j, ic_=ic: e.activation(
                        out=ebB[:, j_ * 512:(j_ + 1) * 512], in_=scor[:, j_ * 512:(j_ + 1) * 512],
                        func=AF.Silu, scale=sgc[:, ic_:ic_ + 1]),
                        inc="aS", waits=flat([("tS", gate_stop)], ev["ebB_free"] if j == 0 else []))
                    silu_ev.append(aa)
                scor_free[0] = [("aS", silu_ev[-1])]
                for kb in range(0, NCH, 8):
                    half = (kb // 8) % 2
                    wrdy = ring_half_load(half, wub_d[icb, :, kb * 512:(kb + 8) * 512])
                    for kc in range(kb, kb + 8):
                        jw = half * 8 + (kc - kb)
                        for j in range(4):
                            w_ = flat(wrdy, scor_free[0] if (kc == kb and j == 0) else [])
                            emit("tensor", lambda e, j_=j, jw_=jw, kc_=kc, sg_=sg: e.matmul(
                                out=scor[:, j_ * 512:(j_ + 1) * 512],
                                lhsT=wbf[:, jw_ * 512 + j_ * 128: jw_ * 512 + j_ * 128 + 128],
                                rhs=xnf[:, kc_ * S + sg_ * 512: kc_ * S + (sg_ + 1) * 512],
                                start=(kc_ == 0), stop=(kc_ == NCH - 1)),
                                inc="tS" if j == 3 else None,
                                waits=w_ if j == 0 else ())
                    ring_free[half] = [("tS", ctr["tS"])]
                up_stop = ctr["tS"]
                hv = []
                for j in range(4):
                    ic = icb * 4 + j
                    vv = emit("vector", lambda e, j_=j, ic_=ic: e.tensor_scalar(
                        out=ebA[:, j_ * 512:(j_ + 1) * 512], in0=scor[:, j_ * 512:(j_ + 1) * 512],
                        scalar1=suc[:, ic_:ic_ + 1], scalar2=None, op0=ALU.mult),
                        inc="vS", waits=flat([("tS", up_stop)], ev["ebA_free"] if j == 0 else []))
                    pp = emit("gpsimd", lambda e, j_=j, ic_=ic: e.tensor_tensor(
                        out=arena[:, F_HT + ic_ * 512: F_HT + (ic_ + 1) * 512],
                        in0=ebB[:, j_ * 512:(j_ + 1) * 512], in1=ebA[:, j_ * 512:(j_ + 1) * 512], op=ALU.mult),
                        inc="pS", waits=[("aS", silu_ev[j]), ("vS", vv)])
                    hv.append(pp)
                scor_free[0] = [("vS", ctr["vS"])]
                ev["ebB_free"] = [("pS", hv[-1])]
                ev["ebA_free"] = [("pS", hv[-1])]
            ev["hT"] = [("pS", ctr["pS"])]
            if sg == 0:
                cA = dma("sync", "eA", bbuf, sdbc_d[:, :], waits=flat(ev["bbuf_free"]))
                ev["ebA_bias"] = [("eA", cA)]
            sb = [sg * 4 + j for j in range(4)]
            for ot in range(4):
                for kb in range(0, NIC, 8):
                    half = (kb // 8) % 2
                    wrdy = ring_half_load(half, wdb_d[ot, :, kb * 512:(kb + 8) * 512])
                    for k in range(kb, kb + 8):
                        jw = half * 8 + (k - kb)
                        for j in range(4):
                            w_ = flat(wrdy, ev["hT"],
                                      scor_free[0] if (k == kb and j == 0) else [])
                            emit("tensor", lambda e, j_=j, jw_=jw, k_=k: e.matmul(
                                out=scor[:, j_ * 512:(j_ + 1) * 512],
                                lhsT=arena[:, F_HT + k_ * 512 + j_ * 128: F_HT + k_ * 512 + j_ * 128 + 128],
                                rhs=wbf[:, jw_ * 512:(jw_ + 1) * 512],
                                start=(k_ == 0), stop=(k_ == NIC - 1)),
                                inc="tS" if j == 3 else None,
                                waits=w_ if j == 0 else ())
                    ring_free[half] = [("tS", ctr["tS"])]
                tl = ctr["tS"]
                for j, s_g in enumerate(sb):
                    evac_stash(s_g, ot, scor[:, j * 512:(j + 1) * 512], "sdown",
                               flat([("tS", tl)], ev["ebA_bias"]))
                scor_free[0] = [("vS", ctr["vS"])]
            ev["ebA_free"] = [("vS", ctr["vS"])]
            for s_g in sb:
                epi_chain(sub, s_g)
        ev["bbuf_free"] = [("vS", ctr["vS"])]
        if sub < 3:
            ev["xnf"] = [("vS", ctr["vS"])]

    def attn_b():
        fence_wbf()
        m_free2[0] = flat(ev["ebA_free"])
        m_free2[1] = flat(ev["ebA_free"])
        rope_free2[0] = [("tS", ctr["tS"])]
        rope_free2[1] = [("tS", ctr["tS"])]
        for kk in range(4):
            qkv_v = []
            pump_until("wattn")
            wrdy = load_blk_all(wabq_d[kk, :, :], 16 * 512, extra=pcw["wattn"])
            for hh in range(4):
                for stt in range(NT):
                    pp = next_psum()
                    tl = proj_accum(pp,
                                    [wbf[:, kc * 512 + hh * 128: kc * 512 + hh * 128 + 128] for kc in range(NCH)],
                                    [xnf[:, kc * S + stt * 512: kc * S + (stt + 1) * 512] for kc in range(NCH)],
                                    extra_waits=flat(wrdy, ev["xnf"]))
                    fe, re = rope_evac(pp, stt, arena[:, B_Q4 + hh * S + stt * 512: B_Q4 + hh * S + (stt + 1) * 512], tl)
                    psum_free[id(pp)] = fe
                    qkv_v.append(re[0][1])
            for kc in range(NCH):
                wbf_free[kc] = [("tS", ctr["tS"])]
            wrdy = load_blk_all(wabk_d[kk, :, :], 16 * 128)
            for stt in range(NT):
                pp = next_psum()
                tl = proj_accum(pp,
                                [wbf[:, kc * 128: kc * 128 + 128] for kc in range(NCH)],
                                [xnf[:, kc * S + stt * 512: kc * S + (stt + 1) * 512] for kc in range(NCH)],
                                extra_waits=wrdy)
                fe, re = rope_evac(pp, stt, arena[:, B_KB + stt * 512: B_KB + (stt + 1) * 512], tl)
                psum_free[id(pp)] = fe
                qkv_v.append(re[0][1])
            for kc in range(NCH):
                wbf_free[kc] = [("tS", ctr["tS"])]
            wrdy = load_blk_all(wabv_d[kk, :, :], 16 * 512)
            for ssub in range(NS):
                pp = next_psum()
                tl = proj_accum(pp,
                                [xnf[:, kc * S + ssub * 128: kc * S + ssub * 128 + 128] for kc in range(NCH)],
                                [wbf[:, kc * 512:(kc + 1) * 512] for kc in range(NCH)],
                                extra_waits=wrdy)
                vv = emit("vector", lambda e, pp_=pp, s_=ssub: e.tensor_copy(
                    out=arena[:, B_V4 + s_ * 512: B_V4 + (s_ + 1) * 512], in_=pp_[:, 0:512]),
                    inc="vS", waits=[("tS", tl)])
                psum_free[id(pp)] = [("vS", vv)]
                qkv_v.append(vv)
            for kc in range(NCH):
                wbf_free[kc] = [("tS", ctr["tS"])]
            ev["qkv"] = [("vS", max(qkv_v))]
            for hh in range(4):
                h = 4 * kk + hh
                for J in range(4):
                    pump(7)
                    for p in range(4):
                        softmax_transpose(4, J, p, False, B_Q4 + hh * S, B_KB, B_PT, B_PR)
                    for dvc in range(4):
                        pp = next_psum()
                        tl = proj_accum(pp,
                                        [arena[:, B_V4 + kc * 512 + dvc * 128: B_V4 + kc * 512 + dvc * 128 + 128] for kc in range(16)],
                                        [arena[:, B_PT + kc * 512: B_PT + (kc + 1) * 512] for kc in range(16)],
                                        extra_waits=flat(probs_free[0], ev["ctxs_free"]))
                        vv = emit("vector", lambda e, pp_=pp, J_=J, d_=dvc: e.tensor_copy(
                            out=arena[:, B_CS + d_ * 2048 + J_ * 512: B_CS + d_ * 2048 + (J_ + 1) * 512],
                            in_=pp_[:, 0:512]),
                            inc="vS", waits=[("tS", tl)])
                        psum_free[id(pp)] = [("vS", vv)]
                cw = None
                for dvc in range(4):
                    cc = h * 4 + dvc
                    for sbq in range(4):
                        cw = dma("gpsimd", "cw", ctxb2_d[sbq, :, cc * 512:(cc + 1) * 512],
                                 arena[:, B_CS + dvc * 2048 + sbq * 512: B_CS + dvc * 2048 + (sbq + 1) * 512],
                                 waits=[("vS", ctr["vS"])])
                ev["ctxs_free"] = [("cw", cw)]
        ev["ebA_free"] = flat(m_free2[0], m_free2[1])

    attn_a()
    ctxa_done = [("cw", ctr["cw"])]
    pump_until("wot")
    run_outproj(16, "plain", wotb_d, ctxa2_d, ctxa_done, None, pcw["wot"], 0)
    pump_until("ffn")
    ffn(1)
    attn_b()
    ctxb_done = [("cw", ctr["cw"])]
    pump_until("wout")
    run_outproj(64, "bias", woutb_d, ctxb2_d, ctxb_done, bbc_d, pcw["wout"], 2)
    ffn(3)

    emit("gpsimd", lambda e: e.memset(cols[:, 15:16], 0.0),
         waits=[(k, ctr[k]) for k in SEM_NAMES if ctr[k] > 0])

    with nc.Block() as block:
        def mk(ename):
            def sec(eng):
                for ws, fn, inc, n in streams[ename]:
                    for (s_, v) in ws:
                        eng.wait_ge(sems[s_], v)
                    inst = fn(eng)
                    if inc is not None:
                        inst.then_inc(sems[inc], n)
            sec.__name__ = ename
            return sec
        block.tensor(mk("tensor"))
        block.vector(mk("vector"))
        block.scalar(mk("scalar"))
        block.gpsimd(mk("gpsimd"))
        block.sync(mk("sync"))
    est.close()
    n_inst = sum(len(v) for v in streams.values())
    return nc, n_inst


def _host_constants():
    HD = 128
    TH = 10000.0
    inv = 1.0 / TH ** (np.arange(0, HD, 2) / HD)
    t = np.arange(S)
    ang_half = np.outer(inv, t)
    ang = np.concatenate([ang_half, ang_half], 0)
    cosb = np.cos(ang).astype(ml_dtypes.bfloat16)
    sin = np.sin(ang)
    sinS = np.concatenate([-sin[:64], sin[64:]], 0).astype(ml_dtypes.bfloat16)
    qq = np.arange(128)[:, None]
    m = np.zeros((128, 2048), dtype=np.float32)
    for p in range(4):
        kkk = np.arange(512)[None, :]
        m[:, p * 512:(p + 1) * 512] = (kkk <= p * 128 + qq).astype(np.float32)
    mask01 = m.astype(ml_dtypes.bfloat16)
    ident = np.eye(128, dtype=np.float32).astype(ml_dtypes.bfloat16)
    return cosb, sinS, mask01, ident


_CACHE = {}


def kernel(L0_attn_a_hidden_states, L0_attn_a_Wq, L0_attn_a_Wk, L0_attn_a_Wv, L0_attn_a_Wo,
           L0_attn_b_W_attn, L0_attn_b_W_out, L0_attn_b_b_out,
           L0_ffn_w_gate, L0_ffn_w_up, L0_ffn_w_down,
           L0_ffn_s_gate, L0_ffn_s_up, L0_ffn_s_down):
    x = np.asarray(L0_attn_a_hidden_states, dtype=np.float32)
    if "nc" not in _CACHE:
        _CACHE["nc"], _CACHE["n"] = build_program()
    nc = _CACHE["nc"]
    cosb, sinS, mask01, ident = _host_constants()
    wqt = np.ascontiguousarray(np.asarray(L0_attn_a_Wq, np.float32).T)
    wkt = np.ascontiguousarray(np.asarray(L0_attn_a_Wk, np.float32).T)
    wvt = np.ascontiguousarray(np.asarray(L0_attn_a_Wv, np.float32).T)
    wot = np.ascontiguousarray(np.asarray(L0_attn_a_Wo, np.float32).T)
    wattn = np.ascontiguousarray(np.asarray(L0_attn_b_W_attn, np.float32))
    wout = np.ascontiguousarray(np.asarray(L0_attn_b_W_out, np.float32))
    wg8 = np.ascontiguousarray(np.asarray(L0_ffn_w_gate, np.int8))
    wu8 = np.ascontiguousarray(np.asarray(L0_ffn_w_up, np.int8))
    wd8 = np.ascontiguousarray(np.asarray(L0_ffn_w_down, np.int8))
    sgc = np.ascontiguousarray(np.asarray(L0_ffn_s_gate, np.float32).reshape(64, 128).T)
    suc = np.ascontiguousarray(np.asarray(L0_ffn_s_up, np.float32).reshape(64, 128).T)
    sdbc = np.ascontiguousarray(np.broadcast_to(np.asarray(L0_ffn_s_down, np.float32)[None, :], (128, D)))
    bbc = np.ascontiguousarray(np.broadcast_to(np.asarray(L0_attn_b_b_out, np.float32)[None, :], (128, D)))
    shared = dict(wqt=wqt, wkt=wkt, wvt=wvt, wot=wot, wattn=wattn, wout=wout,
                  wg8=wg8, wu8=wu8, wd8=wd8, sgc=sgc, suc=suc, sdbc=sdbc, bbc=bbc,
                  cosb=cosb, sinb=sinS, mask01=mask01, ident=ident)
    in_maps = []
    for c in range(8):
        m = dict(shared)
        m["x"] = np.ascontiguousarray(x[c])
        in_maps.append(m)
    res = run_bass_kernel_spmd(nc, in_maps, core_ids=list(range(8)))
    return np.stack([np.asarray(r["out"]) for r in res.results], axis=0)


if __name__ == "__main__":
    nc, n = build_program()
    print("built", n, "instructions")
